# version 03
# baseline (speedup 1.0000x reference)
import contextlib
from collections import defaultdict

import numpy as np
import ml_dtypes

import concourse.bass as bass
import concourse.mybir as mybir
from concourse.alu_op_type import AluOpType
from concourse.bass_utils import run_bass_kernel_spmd

F32 = mybir.dt.float32
BF16 = mybir.dt.bfloat16
AF = mybir.ActivationFunctionType
AX = mybir.AxisListType
BF = ml_dtypes.bfloat16
NEG = -1.0e30

ENGINES = ("pe", "dve", "act", "pool", "sp")
DETECT_RACES = False
TAPS = set()


def real_cfg():
    return dict(D=1024, NH=16, HD=64, L=8192, OWN=2048, HALO=256,
                E=16, G=4, INTER=512, SH_I=1024, H1=4096, CTX=250)


def tiny_cfg():
    return dict(D=256, NH=4, HD=64, L=1024, OWN=256, HALO=128,
                E=4, G=2, INTER=128, SH_I=256, H1=512, CTX=100)


def blocks_of(total, width=512):
    out = []
    s = 0
    rem = total % width
    if rem:
        out.append((0, rem))
        s = rem
    while s < total:
        out.append((s, width))
        s += width
    return out


class Builder:
    def __init__(self, nc):
        self.nc = nc
        self.em = {e: [] for e in ENGINES}
        self.cnt = defaultdict(int)
        self.sems = {}
        self.es = contextlib.ExitStack()
        self.bankg = {}
        self.dma_sems = set()

    def sem(self, name):
        if name not in self.sems:
            self.sems[name] = self.es.enter_context(self.nc.semaphore(name))
        return self.sems[name]

    def op(self, eng, build_fn, inc=None, incby=1, waits=(), auto=True):
        if auto and eng in ("dve", "act"):
            waits = tuple(waits) + ((eng, self.cnt[eng]),)
            if inc is None:
                inc, incby = eng, 1
        waits = tuple((s, v) for (s, v) in waits if v > 0)
        if inc is not None:
            self.sem(inc)
        for s, _ in waits:
            self.sem(s)
        semd = self.sems

        def f(e):
            for s, v in waits:
                e.wait_ge(semd[s], v)
            ins = build_fn(e)
            if inc is not None and ins is not None:
                ins.then_inc(semd[inc], incby)
        self.em[eng].append(f)
        if inc is not None:
            self.cnt[inc] += incby
        return self.cnt[inc] if inc is not None else None

    def w(self, *names):
        return tuple((n, self.cnt[n]) for n in names)

    def barrier(self):
        on = [s for s in (("pe", "dve", "act", "pool") + tuple(sorted(self.dma_sems)))
              if self.cnt[s] > 0]
        waits = self.w(*on)
        for e in ENGINES:
            self.op(e, lambda _e: None, waits=waits, auto=False)

    def bank_wait(self, *banks):
        out = []
        for bk in banks:
            out.extend(self.bankg.get(bk, ()))
        return tuple(out)

    def bank_done(self, bank, sem):
        self.bankg[bank] = ((sem, self.cnt[sem]),)

    def finalize(self, block):
        eng_objs = {"pe": block.tensor, "dve": block.vector, "act": block.scalar,
                    "pool": block.gpsimd, "sp": block.sync}
        for name in ENGINES:
            if not self.em[name]:
                continue

            def make(fns):
                def body(e):
                    for f in fns:
                        f(e)
                return body
            eng_objs[name](make(self.em[name]))


def e_dma(b, sem, out_ap, in_ap, waits=()):
    b.dma_sems.add(sem)
    return b.op("sp", lambda e: e.dma_start(out=out_ap, in_=in_ap),
                inc=sem, incby=16, waits=waits)


def e_mm(b, out_ap, lhsT, rhs, start, stop, waits=(), inc=None):
    return b.op("pe", lambda e: e.matmul(out_ap, lhsT, rhs, start=start, stop=stop),
                waits=waits, inc=inc)


def e_tr(b, out_ap, in_ap, ident, waits=(), inc=None):
    return b.op("pe", lambda e: e.transpose(out_ap, in_ap, ident),
                waits=waits, inc=inc)


def np_bf(x):
    return np.ascontiguousarray(np.asarray(x, dtype=np.float32)).astype(BF)



def build_program(cfg, mode):
    D = cfg["D"]; NH = cfg["NH"]; HD = cfg["HD"]; L = cfg["L"]
    OWN = cfg["OWN"]; HALO = cfg["HALO"]; E = cfg["E"]; G = cfg["G"]
    INTER = cfg["INTER"]; SH_I = cfg["SH_I"]; H1 = cfg["H1"]
    EXT = OWN + HALO
    DCH = D // 128
    NPAIR = NH // 2
    ICH = INTER // 128
    SHCH = SH_I // 128
    H1CH = H1 // 128
    GS = E // G
    NKC = HALO // 128 + 1
    LKB = L // 512
    LC = L // 128
    ext_blocks = blocks_of(EXT)
    own_blocks = blocks_of(OWN)
    full_blocks = blocks_of(L)
    n_ext_tiles = EXT // 128
    n_own_tiles = OWN // 128
    scale0 = float(HD) ** 0.5
    NB = max(1, D // 512)
    OW = min(512, D)
    SB_N = min(5, LC)

    do_l0 = mode in ("l0", "fused")
    do_l1 = mode in ("l1", "fused")

    nc = bass.Bass("TRN2", target_bir_lowering=False,
                   detect_race_conditions=DETECT_RACES,
                   dynamic_dma_scratch_size=4096)
    b = Builder(nc)

    def din(name, shape, dtype=F32):
        return nc.dram_tensor(name, list(shape), dtype, kind="ExternalInput").ap()

    def dscr(name, shape, dtype=BF16):
        return nc.dram_tensor(name, list(shape), dtype).ap()

    ident_bf_d = din("ident_bf", [128, 128], BF16)
    ones_row_bf_d = din("ones_row_bf", [1, 128], BF16)
    ones_row_f_d = din("ones_row_f", [1, 128], F32)

    if do_l0:
        x_full = din("x_full", [L, D])
        x_ext = din("x_ext", [EXT, D])
        wqkv = din("wqkv", [3 * NH, D, HD])
        bq8_col = din("bq8_col", [NPAIR, 128, 1])
        bk_col = din("bk_col", [NPAIR, 128, 1])
        bv_row = din("bv_row", [1, NH * HD])
        wp = din("wp", [D, D])
        bp_row = din("bp_row", [1, D])
        gate_w = din("gate_w", [E, D])
        gb_rep = din("gb_rep", [128, E])
        we1 = din("we1", [E, D, INTER])
        we3 = din("we3", [E, D, INTER])
        we2 = din("we2", [E, INTER, D])
        be1_col = din("be1_col", [E, ICH, 128, 1])
        be3_col = din("be3_col", [E, ICH, 128, 1])
        be2 = din("be2", [E, D])
        ws1 = din("ws1", [D, SH_I])
        ws3 = din("ws3", [D, SH_I])
        ws2 = din("ws2", [SH_I, D])
        bs1_col = din("bs1_col", [SHCH, 128, 1])
        bs3_col = din("bs3_col", [SHCH, 128, 1])
        bs2_row = din("bs2_row", [1, D])
        wqkv_b = dscr("wqkv_b", [3 * NH, D, HD])
        we1_b = dscr("we1_b", [E, D, INTER])
        we3_b = dscr("we3_b", [E, D, INTER])
        we2_b = dscr("we2_b", [E, INTER, D])
        ws1_b = dscr("ws1_b", [D, SH_I])
        ws3_b = dscr("ws3_b", [D, SH_I])
        ws2_b = dscr("ws2_b", [SH_I, D])
        be2_b = dscr("be2_b", [E, D])
        k_scr = dscr("k_scr", [NH, HD, L])
        v_scr = dscr("v_scr", [NH, LC, 128, HD])
        q_scr = dscr("q_scr", [NH, HD, EXT])
        x1_scr = nc.dram_tensor("x1_scr", [EXT, D], F32).ap()
    if do_l1:
        w_in = din("in_proj_w", [3 * D, D])
        w_out = din("out_proj_w", [D, D])
        ropeq_cos = din("ropeq_cos", [EXT, D], BF16)
        ropeq_sin = din("ropeq_sin", [EXT, D], BF16)
        ropek_cos = din("ropek_cos", [EXT, D], BF16)
        ropek_sin = din("ropek_sin", [EXT, D], BF16)
        masks = din("masks", [n_own_tiles, 128, NKC * 128])
        w1 = din("w1", [H1, D])
        b1_col = din("b1_col", [H1CH, 128, 1])
        w2 = din("w2", [D, H1])
        b2_row = din("b2_row", [1, D])
        w1t_scr = dscr("w1t_scr", [DCH, 128, H1])
        w2t_scr = dscr("w2t_scr", [H1CH, 128, D])
        if mode == "l1":
            x1_src = din("x1_ext", [EXT, D])
        else:
            x1_src = x1_scr

    if mode == "l0":
        out_x1 = nc.dram_tensor("x1_own", [OWN, D], F32, kind="ExternalOutput").ap()
    else:
        out_fin = nc.dram_tensor("out_own", [OWN, D], F32, kind="ExternalOutput").ap()

    def sbt(stack, name, shape, dtype=F32):
        return stack.enter_context(nc.sbuf_tensor(name, list(shape), dtype))

    psum_f = b.es.enter_context(nc.psum_tensor("psum_f", [128, 3072], F32))
    psum_bf = b.es.enter_context(nc.psum_tensor("psum_bf", [128, 2048], BF16))

    def pf(bank, width=512):
        return psum_f[:, bank * 512: bank * 512 + width]

    def pbf(half, width=1024):
        return psum_bf[:, half * 1024: half * 1024 + width]

    def cast_to_scr(src2d, dst2d, rows, cols, stg, stg_bf, colch):
        last_cp = [b.cnt["dve"]]
        last_st = [b.cnt["qcs"]]
        for r0 in range(0, rows, 128):
            rn = min(128, rows - r0)
            for c0 in range(0, cols, colch):
                cn = min(colch, cols - c0)
                ld = e_dma(b, "qc", stg[0:rn, 0:cn], src2d[r0:r0 + rn, c0:c0 + cn],
                           waits=(("dve", last_cp[0]),))
                last_cp[0] = b.op(
                    "dve", lambda e, i=stg[0:rn, 0:cn], o=stg_bf[0:rn, 0:cn]:
                    e.tensor_copy(o, i), inc="dve",
                    waits=(("qc", ld), ("qcs", last_st[0])))
                last_st[0] = e_dma(b, "qcs", dst2d[r0:r0 + rn, c0:c0 + cn],
                                   stg_bf[0:rn, 0:cn], waits=(("dve", last_cp[0]),))

    def emit_ln(s, x_src_fn, n_tiles, dest_chunks, dest_col0, dest_guard=0):
        inv_d = 1.0 / D
        for t in range(n_tiles):
            xt = s["x"][t % 2]
            xn = s["xn"][t % 2]
            ld = e_dma(b, f"lq{t % 2}", xt[:, :], x_src_fn(t),
                       waits=(("dve", s["x_free"][t % 2]),))
            b.op("dve", lambda e, a=xt: e.reduce_sum(s["m"][:, :], a[:, :], axis=AX.X),
                 waits=((f"lq{t % 2}", ld),))
            b.op("dve", lambda e: e.tensor_scalar(s["m"][:, :], s["m"][:, :], inv_d,
                                                  None, AluOpType.mult))
            b.op("dve", lambda e, a=xt: e.tensor_scalar(
                s["xc"][:, :], a[:, :], s["m"][:, :], None, AluOpType.subtract),
                inc="dve")
            s["x_free"][t % 2] = b.cnt["dve"]
            c_xc = b.cnt["dve"]
            b.op("act", lambda e: e.activation(s["xsq"][:, :], s["xc"][:, :], AF.Square,
                                               accum_out=s["ssq"][:, :]),
                 waits=(("dve", c_xc),), inc="act")
            c_sq = b.cnt["act"]
            b.op("dve", lambda e: e.tensor_scalar(
                s["ssq"][:, :], s["ssq"][:, :], inv_d, 1e-5,
                AluOpType.mult, AluOpType.add), waits=(("act", c_sq),), inc="dve")
            c_v = b.cnt["dve"]
            b.op("act", lambda e: e.activation(s["sd"][:, :], s["ssq"][:, :], AF.Sqrt),
                 waits=(("dve", c_v),), inc="act")
            c_sd = b.cnt["act"]
            b.op("dve", lambda e: e.reciprocal(s["rstd"][:, :], s["sd"][:, :]),
                 waits=(("act", c_sd),))
            b.op("dve", lambda e, o=xn: e.tensor_scalar(
                o[:, :], s["xc"][:, :], s["rstd"][:, :], None, AluOpType.mult),
                inc="dve", waits=(("pe", s["xn_free"][t % 2]),))
            c_xn = b.cnt["dve"]
            half = t % 2
            for c in range(DCH):
                pslot = pbf(half)[:, c * 128:(c + 1) * 128]
                e_tr(b, pslot, xn[:, c * 128:(c + 1) * 128], s["IB"],
                     waits=((("dve", c_xn),) + b.bank_wait(("bf", half)))
                     if c == 0 else (),
                     inc="pe" if c == DCH - 1 else None)
            s["xn_free"][t % 2] = b.cnt["pe"]
            c_tr = b.cnt["pe"]
            for c in range(DCH):
                b.op("dve", lambda e, p=pbf(half)[:, c * 128:(c + 1) * 128],
                     o=dest_chunks[c][:, dest_col0 + t * 128:dest_col0 + (t + 1) * 128]:
                     e.tensor_copy(o, p),
                     waits=(("pe", c_tr), ("pe", dest_guard)) if c == 0 else (),
                     inc="dve" if c == DCH - 1 else None)
            b.bankg[("bf", half)] = (("dve", b.cnt["dve"]),)

    def make_ln_sb(stk, IB, pre=""):
        return dict(
            x=[sbt(stk, pre + "lnx0", [128, D]), sbt(stk, pre + "lnx1", [128, D])],
            xc=sbt(stk, pre + "lnxc", [128, D]), xsq=sbt(stk, pre + "lnxsq", [128, D]),
            m=sbt(stk, pre + "lnm", [128, 1]), ssq=sbt(stk, pre + "lnssq", [128, 1]),
            sd=sbt(stk, pre + "lnsd", [128, 1]), rstd=sbt(stk, pre + "lnrstd", [128, 1]),
            xn=[sbt(stk, pre + "lnxn0", [128, D], BF16),
                sbt(stk, pre + "lnxn1", [128, D], BF16)],
            x_free=[0, 0], xn_free=[0, 0], IB=IB,
        )

    def prep_T(src_rows_fn, n_row_chunks, dest_tiles, stg, stg_bf, IB, row_w,
               reader_guard=0):
        first = True
        for oc in range(n_row_chunks):
            ld = e_dma(b, "qc", stg[:, 0:row_w], src_rows_fn(oc),
                       waits=(("dve", b.cnt["dve"]),))
            cp = b.op("dve", lambda e, o=stg_bf[:, 0:row_w], i=stg[:, 0:row_w]:
                      e.tensor_copy(o, i), inc="dve",
                      waits=(("qc", ld), ("pe", b.cnt["pe"]),
                             ("qcs", b.cnt["qcs"]), ("qst", b.cnt["qst"])))
            for ic in range(row_w // 128):
                half = ic % 2
                pslot = pbf(half)[:, 0:128]
                tr = e_tr(b, pslot, stg_bf[:, ic * 128:(ic + 1) * 128], IB,
                          waits=(("dve", cp), ("pe", reader_guard if first else 0))
                          + b.bank_wait(("bf", half)), inc="pe")
                first = False
                b.op("dve", lambda e, p=pslot,
                     o=dest_tiles[ic][:, oc * 128:(oc + 1) * 128]:
                     e.tensor_copy(o, p), waits=(("pe", tr),), inc="dve")
                b.bankg[("bf", half)] = (("dve", b.cnt["dve"]),)

    def tap(name, tiles, per_tile_cols=None):
        if name not in TAPS:
            return
        W_ = tiles[0].shape[1] if per_tile_cols is None else per_tile_cols
        o = nc.dram_tensor(f"tap_{name}", [len(tiles), 128, W_], F32,
                           kind="ExternalOutput").ap()
        b.barrier()
        with nc.sbuf_tensor(f"tap_{name}_s", [128, W_], F32) as tapt:
            for i, t_ in enumerate(tiles):
                c = b.op("dve", lambda e, t_=t_: e.tensor_copy(tapt[:, :], t_[:, 0:W_]),
                         inc="dve", waits=(("qtap", b.cnt["qtap"]),))
                e_dma(b, "qtap", o[i], tapt[:, :], waits=(("dve", c),))
            b.barrier()

    with b.es:
        ident_bf = sbt(b.es, "ident_s", [128, 128], BF16)
        ones_row_bf = sbt(b.es, "ones_bf_s", [1, 128], BF16)
        ones_row_f = sbt(b.es, "ones_f_s", [1, 128], F32)
        for d_ap, s_t in ((ident_bf_d, ident_bf), (ones_row_bf_d, ones_row_bf),
                          (ones_row_f_d, ones_row_f)):
            e_dma(b, "qc", s_t[:, :], d_ap)
        stg = sbt(b.es, "stg", [128, 2048], F32)
        stg_bf = sbt(b.es, "stg_bf", [128, 4096], BF16)
        IB = ident_bf[:, :]

        xn_ext = [sbt(b.es, f"xne{c}", [128, EXT], BF16) for c in range(DCH)]

        if do_l0:
            l0s = contextlib.ExitStack()
            comb = sbt(l0s, "comb", [128, n_ext_tiles * E])
            av_fm = [sbt(l0s, f"avfm{p}", [128, EXT], BF16) for p in range(NPAIR)]
            wpT = [sbt(l0s, f"wpT{c}", [128, D], BF16) for c in range(DCH)]
            gwT = [sbt(l0s, f"gwT{c}", [128, E], BF16) for c in range(DCH)]
            gbr = sbt(l0s, "gbr", [128, E])
            bvr = sbt(l0s, "bvr", [1, NH * HD], BF16)
            bpsr = sbt(l0s, "bpsr", [1, D], BF16)
            bq8c = sbt(l0s, "bq8c", [128, NPAIR])
            bkc = sbt(l0s, "bkc", [128, NPAIR])

            if "skip_casts" not in TAPS:
              cast_to_scr(wqkv.rearrange("e d c -> (e d) c"),
                        wqkv_b.rearrange("e d c -> (e d) c"),
                        3 * NH * D, HD, stg, stg_bf, HD)
              for src, dst, rr, cc in (
                (we1.rearrange("e d i -> (e d) i"), we1_b.rearrange("e d i -> (e d) i"), E * D, INTER),
                (we3.rearrange("e d i -> (e d) i"), we3_b.rearrange("e d i -> (e d) i"), E * D, INTER),
                (we2.rearrange("e i d -> (e i) d"), we2_b.rearrange("e i d -> (e i) d"), E * INTER, D),
                (ws1, ws1_b, D, SH_I), (ws3, ws3_b, D, SH_I), (ws2, ws2_b, SH_I, D),
                (be2, be2_b, E, D),
              ):
                cast_to_scr(src, dst, rr, cc, stg, stg_bf, min(2048, cc))
            if "skip_prept" not in TAPS:
              prep_T(lambda oc: wp[oc * 128:(oc + 1) * 128, :], DCH, wpT,
                   stg, stg_bf, IB, D)
            if "skip_gatebias" not in TAPS:
              _gb_sentinel = 1
            ld = e_dma(b, "qc", stg[0:E, 0:D], gate_w, waits=(("dve", b.cnt["dve"]),))
            cp = b.op("dve", lambda e: e.tensor_copy(stg_bf[0:E, 0:D], stg[0:E, 0:D]),
                      inc="dve", waits=(("qc", ld), ("qcs", b.cnt["qcs"])))
            for c in range(DCH):
                half = c % 2
                pslot = pbf(half)[:, 0:E]
                tr = e_tr(b, pslot, stg_bf[0:E, c * 128:(c + 1) * 128],
                          ident_bf[0:E, 0:E],
                          waits=(("dve", cp),) + b.bank_wait(("bf", half)), inc="pe")
                b.op("dve", lambda e, p=pslot, c=c: e.tensor_copy(gwT[c][:, :], p),
                     waits=(("pe", tr),), inc="dve")
                b.bankg[("bf", half)] = (("dve", b.cnt["dve"]),)
            e_dma(b, "qc", gbr[:, :], gb_rep)
            ldv = e_dma(b, "qc", stg[0:1, 0:NH * HD], bv_row,
                        waits=(("dve", b.cnt["dve"]),))
            b.op("dve", lambda e: e.tensor_copy(bvr[:, :], stg[0:1, 0:NH * HD]),
                 inc="dve", waits=(("qc", ldv),))
            ld1 = e_dma(b, "qc", stg[0:1, 0:D], bp_row, waits=(("dve", b.cnt["dve"]),))
            ld2 = e_dma(b, "qc", stg[0:1, D:2 * D], bs2_row)
            b.op("dve", lambda e: e.tensor_tensor(stg[0:1, 0:D], stg[0:1, 0:D],
                                                  stg[0:1, D:2 * D], AluOpType.add),
                 waits=(("qc", max(ld1, ld2)),))
            b.op("dve", lambda e: e.tensor_copy(bpsr[:, :], stg[0:1, 0:D]), inc="dve")
            for p in range(NPAIR):
                e_dma(b, "qc", bq8c[:, p:p + 1], bq8_col[p])
                e_dma(b, "qc", bkc[:, p:p + 1], bk_col[p])
            b.barrier()

            if "skip_p1" in TAPS:
                pass
            elif True:
              with contextlib.ExitStack() as p1:
                wq_s = [[sbt(p1, f"wq{p}_{c}", [128, 128], BF16) for c in range(DCH)]
                        for p in range(NPAIR)]
                wk_s = [[sbt(p1, f"wk{p}_{c}", [128, 128], BF16) for c in range(DCH)]
                        for p in range(NPAIR)]
                n_vg = max(1, (NH * HD) // 512)
                vgw = (NH * HD) // n_vg
                hpg = vgw // HD
                wv_s = [[sbt(p1, f"wv{g}_{c}", [128, vgw], BF16) for c in range(DCH)]
                        for g in range(n_vg)]
                for p in range(NPAIR):
                    for c in range(DCH):
                        e_dma(b, "qc", wq_s[p][c][:, :].rearrange(
                                  "d (e c) -> d e c", e=2),
                              wqkv_b[2 * p:2 * p + 2, c * 128:(c + 1) * 128, :]
                              .rearrange("e d c -> d e c"))
                        e_dma(b, "qc", wk_s[p][c][:, :].rearrange(
                                  "d (e c) -> d e c", e=2),
                              wqkv_b[NH + 2 * p:NH + 2 * p + 2,
                                     c * 128:(c + 1) * 128, :]
                              .rearrange("e d c -> d e c"))
                for g in range(n_vg):
                    for c in range(DCH):
                        e_dma(b, "qc", wv_s[g][c][:, :].rearrange(
                                  "d (e c) -> d e c", c=HD),
                              wqkv_b[2 * NH + g * hpg:2 * NH + (g + 1) * hpg,
                                     c * 128:(c + 1) * 128, :]
                              .rearrange("e d c -> d e c"))
                ln = make_ln_sb(p1, IB)
                xn_blk = [sbt(p1, f"xnb{c}", [128, 512], BF16) for c in range(DCH)]
                kvs = [sbt(p1, f"kvs{i}", [128, 512], BF16) for i in range(2)]
                kvs_free = [0, 0]
                xnblk_guard = [0]
                b.barrier()

                def kqv_block(bs, bw, dest, dest_c0, is_q):
                    c_ln = b.cnt["dve"]
                    for p in range(NPAIR):
                        bank = p % 4
                        pb = pf(bank, bw)
                        ws = wq_s if is_q else wk_s
                        for c in range(DCH):
                            e_mm(b, pb, ws[p][c][:, :], dest[c][:, dest_c0:dest_c0 + bw],
                                 start=(c == 0), stop=(c == DCH - 1),
                                 waits=((("dve", c_ln),) + b.bank_wait(bank))
                                 if c == 0 else (),
                                 inc="pe" if c == DCH - 1 else None)
                        c_pe = b.cnt["pe"]
                        ki = p % 2
                        st = kvs[ki]
                        for hh in range(2):
                            if is_q:
                                b.op("dve", lambda e, src=pb[hh * 64:(hh + 1) * 64, 0:bw],
                                     o=st[hh * 64:(hh + 1) * 64, 0:bw],
                                     s1=bq8c[hh * 64:(hh + 1) * 64, p:p + 1]:
                                     e.tensor_scalar(o, src, scale0, s1,
                                                     AluOpType.mult, AluOpType.add),
                                     inc="dve",
                                     waits=(("pe", c_pe), (f"kst{ki}", kvs_free[ki]))
                                     if hh == 0 else ())
                            else:
                                b.op("dve", lambda e, src=pb[hh * 64:(hh + 1) * 64, 0:bw],
                                     o=st[hh * 64:(hh + 1) * 64, 0:bw],
                                     s1=bkc[hh * 64:(hh + 1) * 64, p:p + 1]:
                                     e.tensor_scalar(o, src, s1, None, AluOpType.add),
                                     inc="dve",
                                     waits=(("pe", c_pe), (f"kst{ki}", kvs_free[ki]))
                                     if hh == 0 else ())
                        b.bank_done(bank, "dve")
                        c_ev = b.cnt["dve"]
                        scr = q_scr if is_q else k_scr
                        for hh in range(2):
                            e_dma(b, f"kst{ki}", scr[2 * p + hh, :, bs:bs + bw],
                                  st[hh * 64:(hh + 1) * 64, 0:bw],
                                  waits=(("dve", c_ev),) if hh == 0 else ())
                        kvs_free[ki] = b.cnt[f"kst{ki}"]

                def v_block(bs, bw):
                    c_ln = b.cnt["dve"]
                    nt = bw // 128
                    for t in range(nt):
                        for g in range(n_vg):
                            bank = (t * n_vg + g) % 4
                            pb = pf(bank, vgw)
                            for c in range(DCH):
                                e_mm(b, pb, xn_blk[c][:, t * 128:(t + 1) * 128],
                                     wv_s[g][c][:, :], start=(c == 0), stop=False,
                                     waits=((("dve", c_ln),) + b.bank_wait(bank))
                                     if c == 0 else ())
                            e_mm(b, pb, ones_row_bf[:, :], bvr[:, g * vgw:(g + 1) * vgw],
                                 start=False, stop=True, inc="pe")
                            c_pe = b.cnt["pe"]
                            ki = (t * n_vg + g) % 2
                            st = kvs[ki]
                            b.op("dve", lambda e, pb=pb, o=st[:, 0:vgw]:
                                 e.tensor_copy(o, pb), inc="dve",
                                 waits=(("pe", c_pe), (f"kst{ki}", kvs_free[ki])))
                            b.bank_done(bank, "dve")
                            c_ev = b.cnt["dve"]
                            ci = (bs + t * 128) // 128
                            for hh in range(hpg):
                                e_dma(b, f"kst{ki}", v_scr[g * hpg + hh, ci, :, :],
                                      st[:, hh * HD:(hh + 1) * HD],
                                      waits=(("dve", c_ev),) if hh == 0 else ())
                            kvs_free[ki] = b.cnt[f"kst{ki}"]

                for (bs, bw) in full_blocks:
                    emit_ln(ln, lambda t, bs=bs: x_full[bs + t * 128:bs + (t + 1) * 128, :],
                            bw // 128, xn_blk, 0, dest_guard=xnblk_guard[0])
                    kqv_block(bs, bw, xn_blk, 0, is_q=False)
                    v_block(bs, bw)
                    xnblk_guard[0] = b.cnt["pe"]

                gones = sbt(p1, "gones", [128, max(GS, E)])
                b.op("dve", lambda e: e.memset(gones[:, :], 1.0), inc="dve")
                gs_ = dict(
                    p=sbt(p1, "g_p", [128, E]), nm=sbt(p1, "g_nm", [128, 1]),
                    l=sbt(p1, "g_l", [128, 1]), rl=sbt(p1, "g_rl", [128, 1]),
                    sb_=sbt(p1, "g_sb", [128, E]),
                    pair=sbt(p1, "g_pair", [128, G * (GS * (GS - 1) // 2)]),
                    gsc=sbt(p1, "g_gsc", [128, G]), cnt=sbt(p1, "g_cnt", [128, G]),
                    t=sbt(p1, "g_t", [128, E]), keep=sbt(p1, "g_keep", [128, G]),
                    k16=sbt(p1, "g_k16", [128, E]), m1=sbt(p1, "g_m1", [128, 1]),
                    m2=sbt(p1, "g_m2", [128, 1]), msb=sbt(p1, "g_msb", [128, E]),
                )

                def gate_tile(ti):
                    s = gs_
                    pg = pf(5, E)
                    c0 = ti * 128
                    for c in range(DCH):
                        e_mm(b, pg, xn_ext[c][:, c0:c0 + 128], gwT[c][:, :],
                             start=(c == 0), stop=(c == DCH - 1),
                             waits=b.bank_wait(5) if c == 0 else (),
                             inc="pe" if c == DCH - 1 else None)
                    c_pe = b.cnt["pe"]
                    b.op("dve", lambda e: e.reduce_max(s["nm"][:, :], pg, axis=AX.X,
                                                       negate=True),
                         waits=(("pe", c_pe),), inc="dve")
                    b.op("act", lambda e: e.activation(s["p"][:, :], pg, AF.Exp,
                                                       bias=s["nm"][:, :],
                                                       accum_out=s["l"][:, :]),
                         waits=(("dve", b.cnt["dve"]),), inc="act")
                    b.bank_done(5, "act")
                    b.op("dve", lambda e: e.reciprocal(s["rl"][:, :], s["l"][:, :]),
                         waits=(("act", b.cnt["act"]),))
                    b.op("dve", lambda e: e.tensor_scalar(
                        s["p"][:, :], s["p"][:, :], s["rl"][:, :], None, AluOpType.mult))
                    b.op("dve", lambda e: e.tensor_tensor(
                        s["sb_"][:, :], s["p"][:, :], gbr[:, :], AluOpType.add))
                    npr = GS * (GS - 1) // 2
                    sb3 = s["sb_"][:, :].rearrange("p (g s) -> p g s", s=GS)
                    pi = 0
                    for i in range(GS):
                        for j in range(i + 1, GS):
                            b.op("dve", lambda e, i=i, j=j, pi=pi: e.tensor_tensor(
                                s["pair"][:, pi * G:(pi + 1) * G], sb3[:, :, i],
                                sb3[:, :, j], AluOpType.add))
                            pi += 1
                    b.op("dve", lambda e: e.tensor_copy(s["gsc"][:, :], s["pair"][:, 0:G]))
                    for pi in range(1, npr):
                        b.op("dve", lambda e, pi=pi: e.tensor_tensor(
                            s["gsc"][:, :], s["gsc"][:, :],
                            s["pair"][:, pi * G:(pi + 1) * G], AluOpType.max))
                    b.op("dve", lambda e: e.memset(s["cnt"][:, :], 0.0))
                    for gg in range(G):
                        b.op("dve", lambda e, gg=gg: e.tensor_scalar(
                            s["t"][:, 0:G], s["gsc"][:, :], s["gsc"][:, gg:gg + 1],
                            None, AluOpType.is_lt))
                        b.op("dve", lambda e: e.tensor_tensor(
                            s["cnt"][:, :], s["cnt"][:, :], s["t"][:, 0:G],
                            AluOpType.add))
                    b.op("dve", lambda e: e.tensor_scalar(
                        s["keep"][:, :], s["cnt"][:, :], 2.0, None, AluOpType.is_lt))
                    for gg in range(G):
                        b.op("dve", lambda e, gg=gg: e.tensor_scalar(
                            s["k16"][:, gg * GS:(gg + 1) * GS], gones[:, 0:GS],
                            s["keep"][:, gg:gg + 1], None, AluOpType.mult))
                    b.op("dve", lambda e: e.tensor_scalar(
                        s["t"][:, 0:E], s["k16"][:, :], 1.0, -NEG,
                        AluOpType.subtract, AluOpType.mult))
                    b.op("dve", lambda e: e.tensor_tensor(
                        s["msb"][:, :], s["sb_"][:, :], s["t"][:, 0:E], AluOpType.add))
                    b.op("dve", lambda e: e.reduce_max(s["m1"][:, :], s["msb"][:, :],
                                                       axis=AX.X))
                    b.op("dve", lambda e: e.tensor_scalar(
                        s["t"][:, 0:E], s["msb"][:, :], s["m1"][:, :], None,
                        AluOpType.is_ge))
                    b.op("dve", lambda e: e.tensor_scalar(
                        s["t"][:, 0:E], s["t"][:, 0:E], NEG, None, AluOpType.mult))
                    b.op("dve", lambda e: e.tensor_tensor(
                        s["t"][:, 0:E], s["msb"][:, :], s["t"][:, 0:E], AluOpType.add))
                    b.op("dve", lambda e: e.reduce_max(s["m2"][:, :], s["t"][:, 0:E],
                                                       axis=AX.X))
                    b.op("dve", lambda e: e.tensor_scalar(
                        s["t"][:, 0:E], s["msb"][:, :], s["m2"][:, :], None,
                        AluOpType.is_ge))
                    b.op("dve", lambda e, ti=ti: e.tensor_tensor(
                        comb[:, ti * E:(ti + 1) * E], s["p"][:, :], s["t"][:, 0:E],
                        AluOpType.mult), inc="dve")

                for (bs, bw) in ext_blocks:
                    emit_ln(ln, lambda t, bs=bs: x_ext[bs + t * 128:bs + (t + 1) * 128, :],
                            bw // 128, xn_ext, bs)
                    kqv_block(bs, bw, xn_ext, bs, is_q=True)
                    for t in range(bw // 128):
                        gate_tile((bs + t * 128) // 128)
                b.barrier()
            tap("xn0", xn_ext)
            tap("comb", [comb])

            if "skip_p2" in TAPS:
                pass
            elif True:
              with contextlib.ExitStack() as p2:
                kbuf = [sbt(p2, f"kbuf{i}", [65, L], BF16) for i in range(2)]
                vbuf = [sbt(p2, f"vbuf{i}", [128, LC * 65], BF16) for i in range(2)]
                qbuf = [sbt(p2, f"qbuf{i}", [65, EXT], BF16) for i in range(2)]
                pP = [sbt(p2, f"pP{i}", [128, SB_N * 512], BF16) for i in range(2)]
                mscr = sbt(p2, "mscr", [128, LKB])
                nmv = sbt(p2, "nmv", [128, 1])
                nmb = sbt(p2, "nmb", [128, 1], BF16)
                mrow = sbt(p2, "mrow", [1, 128], BF16)
                rl65 = sbt(p2, "rl65", [65, 512])
                rl0 = sbt(p2, "rl0", [1, 512])
                repl = sbt(p2, "repl", [64, 512])
                buf_free = [0, 0]
                buf_ready = [(0, 0), (0, 0)]
                nmb_free = [0]
                mrow_free = [0]
                mrow_done = [0, 0]

                def load_head(h, bi):
                    w0 = (("pe", buf_free[bi]), ("qmrow", b.cnt["qmrow"]))
                    qh = f"qh{bi}"
                    e_dma(b, qh, kbuf[bi][0:HD, :], k_scr[h], waits=w0)
                    for ci in range(LC):
                        e_dma(b, qh, vbuf[bi][:, ci * 65:ci * 65 + HD], v_scr[h, ci])
                    e_dma(b, qh, qbuf[bi][0:HD, :], q_scr[h])
                    c_q = b.cnt[qh]
                    b.op("dve", lambda e, bi=bi: e.memset(kbuf[bi][64:65, :], 1.0),
                         waits=((qh, c_q),), inc="dve")
                    b.op("dve", lambda e, bi=bi: e.memset(
                        vbuf[bi][:, :].rearrange("p (c w) -> p c w", w=65)[:, :, 64:65],
                        1.0), inc="dve")
                    return (c_q, b.cnt["dve"])

                buf_ready[0] = load_head(0, 0)
                for h in range(NH):
                    bi = h % 2
                    if h + 1 < NH:
                        buf_ready[(h + 1) % 2] = load_head(h + 1, (h + 1) % 2)
                    rq, rd = buf_ready[bi]
                    Kb, Vb, Qb = kbuf[bi], vbuf[bi], qbuf[bi]
                    pair, rsel = h // 2, h % 2
                    for (qs_, qw) in ext_blocks:
                        nqt = qw // 128
                        for t in range(nqt):
                            q0 = qs_ + t * 128
                            for kb in range(LKB):
                                bank = kb % 2
                                pa = pf(bank)
                                mm = e_mm(b, pa, Qb[0:HD, q0:q0 + 128],
                                          Kb[0:HD, kb * 512:(kb + 1) * 512],
                                          start=True, stop=True,
                                          waits=((f"qh{bi}", rq), ("dve", rd),
                                                 ("qmrow", mrow_done[bi]))
                                          + b.bank_wait(bank), inc="pe")
                                b.op("dve", lambda e, pa=pa, kb=kb: e.reduce_max(
                                    mscr[:, kb:kb + 1], pa, axis=AX.X),
                                    waits=(("pe", mm),), inc="dve")
                                b.bank_done(bank, "dve")
                            b.op("dve", lambda e: e.reduce_max(
                                nmv[:, :], mscr[:, 0:LKB], axis=AX.X, negate=True))
                            b.op("dve", lambda e: e.tensor_copy(nmb[:, :], nmv[:, :]),
                                 inc="dve", waits=(("pe", nmb_free[0]),))
                            c_nm = b.cnt["dve"]
                            ptm = pbf(0)[0:1, 0:128]
                            tr = e_tr(b, ptm, nmb[:, :], IB,
                                      waits=(("dve", c_nm),) + b.bank_wait(("bf", 0)),
                                      inc="pe")
                            nmb_free[0] = tr
                            cpm = b.op("dve", lambda e: e.tensor_copy(mrow[:, :], ptm),
                                       waits=(("pe", tr), ("qmrow", mrow_free[0])),
                                       inc="dve")
                            b.bankg[("bf", 0)] = (("dve", b.cnt["dve"]),)
                            e_dma(b, "qmrow", Qb[64:65, q0:q0 + 128], mrow[:, :],
                                  waits=(("dve", cpm),))
                            mrow_free[0] = b.cnt["qmrow"]
                        c_qrow = b.cnt["qmrow"]
                        av_ps = psum_f[0:65, 5 * 512:5 * 512 + qw]
                        n_sc = (LC + SB_N - 1) // SB_N
                        first = True
                        for sc in range(n_sc):
                            nch = min(SB_N, LC - sc * SB_N)
                            pb_t = pP[sc % 2]
                            mm_last = None
                            for i in range(nch):
                                cc = sc * SB_N + i
                                ww = ()
                                if i == 0:
                                    ww = (("qmrow", c_qrow),) + b.bank_wait(0, 1, 2, 3, 4)
                                mm_last = e_mm(b, pf(i, qw),
                                               Kb[0:65, cc * 128:(cc + 1) * 128],
                                               Qb[0:65, qs_:qs_ + qw],
                                               start=True, stop=True, waits=ww,
                                               inc="pe" if i == nch - 1 else None)
                            if qw == 512:
                                b.op("act", lambda e, n=nch, o=pP[sc % 2]: e.activation(
                                    o[:, 0:n * 512], psum_f[:, 0:n * 512], AF.Exp),
                                    waits=(("pe", mm_last),), inc="act")
                            else:
                                for i in range(nch):
                                    b.op("act", lambda e, i=i, qw=qw, o=pP[sc % 2]:
                                         e.activation(o[:, i * qw:(i + 1) * qw],
                                                      pf(i, qw), AF.Exp),
                                         waits=(("pe", mm_last),) if i == 0 else (),
                                         inc="act")
                            c_ex = b.cnt["act"]
                            for bk in range(nch):
                                b.bank_done(bk, "act")
                            for i in range(nch):
                                cc = sc * SB_N + i
                                e_mm(b, av_ps, Vb[:, cc * 65:cc * 65 + 65],
                                     pb_t[:, i * qw:(i + 1) * qw],
                                     start=first, stop=(cc == LC - 1),
                                     waits=((("act", c_ex),)
                                            + (b.bank_wait(5) if first else ()))
                                     if i == 0 else (),
                                     inc="pe" if i == nch - 1 else None)
                                first = False
                        c_av = b.cnt["pe"]
                        cr = b.op("dve", lambda e, av_ps=av_ps, qw=qw: e.reciprocal(
                            rl65[64:65, 0:qw], av_ps[64:65, 0:qw]),
                            waits=(("pe", c_av),), inc="dve")
                        dmr = e_dma(b, "qrl", rl0[0:1, 0:qw], rl65[64:65, 0:qw],
                                    waits=(("dve", cr),))
                        rp = pf(0, qw)[0:64, 0:qw]
                        mmr = e_mm(b, rp, ones_row_f[0:1, 0:64], rl0[0:1, 0:qw],
                                   start=True, stop=True,
                                   waits=(("qrl", dmr),) + b.bank_wait(0), inc="pe")
                        ce = b.op("dve", lambda e, rp=rp, qw=qw: e.tensor_copy(
                                  repl[:, 0:qw], rp),
                                  waits=(("pe", mmr),), inc="dve")
                        b.bank_done(0, "dve")
                        b.op("dve", lambda e, av_ps=av_ps, pair=pair, rsel=rsel,
                             qs_=qs_, qw=qw: e.tensor_tensor(
                                 av_fm[pair][rsel * 64:(rsel + 1) * 64, qs_:qs_ + qw],
                                 av_ps[0:64, 0:qw], repl[0:64, 0:qw], AluOpType.mult),
                             waits=(("dve", ce),), inc="dve")
                        b.bank_done(5, "dve")
                    buf_free[bi] = b.cnt["pe"]
                b.barrier()
            tap("av0", av_fm)

            if "skip_p3" in TAPS:
                pass
            elif True:
              with contextlib.ExitStack() as p3:
                n_grp = 2 if n_ext_tiles > 8 else 1
                if n_grp == 2:
                    g_split = (len(ext_blocks) + 1) // 2
                    blk_groups = [ext_blocks[:g_split], ext_blocks[g_split:]]
                else:
                    blk_groups = [ext_blocks]
                max_gt = max(sum(bw for _, bw in g) for g in blk_groups) // 128
                y_acc = [sbt(p3, f"yacc{t}", [128, D]) for t in range(max_gt)]
                combT = sbt(p3, "combT", [E, EXT], BF16)
                ctf = sbt(p3, "ctf", [128, E], BF16)
                be2s = sbt(p3, "be2s", [E, D], BF16)
                ldb = e_dma(b, "qc", be2s[:, :], be2_b,
                            waits=(("dve", b.cnt["dve"]),))
                for t in range(n_ext_tiles):
                    b.op("dve", lambda e, t=t: e.tensor_copy(
                        ctf[:, :], comb[:, t * E:(t + 1) * E]), inc="dve")
                    c1 = b.cnt["dve"]
                    half = t % 2
                    pslot = pbf(half)[0:E, 0:128]
                    tr = e_tr(b, pslot, ctf[:, :], IB,
                              waits=(("dve", c1),) + b.bank_wait(("bf", half)),
                              inc="pe")
                    b.op("dve", lambda e, t=t, p=pslot: e.tensor_copy(
                        combT[:, t * 128:(t + 1) * 128], p), waits=(("pe", tr),),
                        inc="dve")
                    b.bankg[("bf", half)] = (("dve", b.cnt["dve"]),)

                be1s = sbt(p3, "be1s", [128, E * ICH])
                be3s = sbt(p3, "be3s", [128, E * ICH])
                bs1s = sbt(p3, "bs1s", [128, SHCH])
                bs3s = sbt(p3, "bs3s", [128, SHCH])
                for e_i in range(E):
                    for ic in range(ICH):
                        e_dma(b, "qc", be1s[:, e_i * ICH + ic:e_i * ICH + ic + 1],
                              be1_col[e_i, ic])
                        e_dma(b, "qc", be3s[:, e_i * ICH + ic:e_i * ICH + ic + 1],
                              be3_col[e_i, ic])
                for ic in range(SHCH):
                    e_dma(b, "qc", bs1s[:, ic:ic + 1], bs1_col[ic])
                    e_dma(b, "qc", bs3s[:, ic:ic + 1], bs3_col[ic])
                c_bias_ld = b.cnt["qc"]
                w1t = [sbt(p3, f"ew1_{c}", [128, max(INTER, 128)], BF16)
                       for c in range(DCH)]
                w3t = [sbt(p3, f"ew3_{c}", [128, max(INTER, 128)], BF16)
                       for c in range(DCH)]
                w2t = [sbt(p3, f"ew2_{c}", [128, D], BF16)
                       for c in range(max(ICH, 1))]
                hsil = sbt(p3, "hsil", [128, 512])
                het = [sbt(p3, f"het{c}", [128, 512], BF16) for c in range(max(ICH, 1))]
                xres = sbt(p3, "xres", [128, D])
                w_free = [0]
                het_free = [0]
                xres_free = [0]
                for grp in blk_groups:
                  if not grp:
                    continue
                  g0 = grp[0][0] // 128
                  g_tiles = sum(bw for _, bw in grp) // 128
                  for gt in range(g_tiles):
                    t = g0 + gt
                    for half in range(NB):
                        bank = (gt * NB + half) % 4
                        po = pf(bank, OW)
                        for p in range(NPAIR):
                            e_mm(b, po, av_fm[p][:, t * 128:(t + 1) * 128],
                                 wpT[p][:, half * OW:(half + 1) * OW],
                                 start=(p == 0), stop=False,
                                 waits=b.bank_wait(bank) if p == 0 else ())
                        e_mm(b, po, combT[:, t * 128:(t + 1) * 128],
                             be2s[:, half * OW:(half + 1) * OW], start=False,
                             stop=False,
                             waits=(("qc", c_bias_ld), ("dve", b.cnt["dve"]))
                             if (t == g0 and half == 0) else ())
                        e_mm(b, po, ones_row_bf[:, :], bpsr[:, half * OW:(half + 1) * OW],
                             start=False, stop=True, inc="pe")
                        c_pe = b.cnt["pe"]
                        b.op("dve", lambda e, po=po, gt=gt, half=half: e.tensor_copy(
                            y_acc[gt][:, half * OW:(half + 1) * OW], po),
                            waits=(("pe", c_pe), ("qst", b.cnt["qst"])), inc="dve")
                        b.bank_done(bank, "dve")

                  def ffn_unit(n_ich, w1src, w3src, w2src, b1c, b3c, c_col_fn,
                               grp=grp, g0=g0):
                    for c in range(DCH):
                        e_dma(b, "qwu", w1t[c][:, 0:n_ich * 128], w1src(c),
                              waits=(("pe", w_free[0]),) if c == 0 else ())
                        e_dma(b, "qwu", w3t[c][:, 0:n_ich * 128], w3src(c))
                    for ic in range(n_ich):
                        e_dma(b, "qwu", w2t[ic][:, :], w2src(ic))
                    c_w = b.cnt["qwu"]
                    for (bs, bw) in grp:
                        for ic in range(n_ich):
                            b1 = pf(4, bw)
                            b3 = pf(5, bw)
                            for c in range(DCH):
                                e_mm(b, b1, w1t[c][:, ic * 128:(ic + 1) * 128],
                                     xn_ext[c][:, bs:bs + bw], start=(c == 0),
                                     stop=(c == DCH - 1),
                                     waits=((("qwu", c_w),) + b.bank_wait(4))
                                     if c == 0 else ())
                            for c in range(DCH):
                                e_mm(b, b3, w3t[c][:, ic * 128:(ic + 1) * 128],
                                     xn_ext[c][:, bs:bs + bw], start=(c == 0),
                                     stop=(c == DCH - 1),
                                     waits=b.bank_wait(5) if c == 0 else (),
                                     inc="pe" if c == DCH - 1 else None)
                            c_pe = b.cnt["pe"]
                            b.op("act", lambda e, b1=b1, ic=ic, bw=bw: e.activation(
                                hsil[:, 0:bw], b1, AF.Silu, bias=b1c(ic)),
                                waits=(("pe", c_pe), ("dve", b.cnt["dve"]),
                                       ("qc", c_bias_ld)), inc="act")
                            c_si = b.cnt["act"]
                            b.bank_done(4, "act")
                            b.op("dve", lambda e, b3=b3, ic=ic, bw=bw:
                                 e.scalar_tensor_tensor(
                                     het[ic][:, 0:bw], b3, b3c(ic), hsil[:, 0:bw],
                                     AluOpType.add, AluOpType.mult),
                                 waits=(("act", c_si), ("pe", het_free[0])),
                                 inc="dve")
                            b.bank_done(5, "dve")
                        c_h = b.cnt["dve"]
                        nt = bw // 128
                        for tt in range(nt):
                            ti = (bs + tt * 128) // 128
                            gt = ti - g0
                            for half in range(NB):
                                bank = (gt * NB + half) % 4
                                po = pf(bank, OW)
                                for ic in range(n_ich):
                                    e_mm(b, po, het[ic][:, tt * 128:(tt + 1) * 128],
                                         w2t[ic][:, half * OW:(half + 1) * OW],
                                         start=(ic == 0), stop=(ic == n_ich - 1),
                                         waits=((("dve", c_h),) + b.bank_wait(bank))
                                         if ic == 0 else (),
                                         inc="pe" if ic == n_ich - 1 else None)
                                c_pe = b.cnt["pe"]
                                ccol = c_col_fn(ti)
                                if ccol is None:
                                    b.op("dve", lambda e, po=po, gt=gt, half=half:
                                         e.tensor_tensor(
                                             y_acc[gt][:, half * OW:(half + 1) * OW],
                                             y_acc[gt][:, half * OW:(half + 1) * OW],
                                             po, AluOpType.add),
                                         waits=(("pe", c_pe),), inc="dve")
                                else:
                                    b.op("dve", lambda e, po=po, gt=gt, half=half,
                                         ccol=ccol: e.scalar_tensor_tensor(
                                             y_acc[gt][:, half * OW:(half + 1) * OW],
                                             po, ccol,
                                             y_acc[gt][:, half * OW:(half + 1) * OW],
                                             AluOpType.mult, AluOpType.add),
                                         waits=(("pe", c_pe),), inc="dve")
                                b.bank_done(bank, "dve")
                        het_free[0] = b.cnt["pe"]
                    w_free[0] = b.cnt["pe"]

                  for e_i in range(E):
                    ffn_unit(
                        ICH,
                        lambda c, e_i=e_i: we1_b[e_i, c * 128:(c + 1) * 128, :],
                        lambda c, e_i=e_i: we3_b[e_i, c * 128:(c + 1) * 128, :],
                        lambda ic, e_i=e_i: we2_b[e_i, ic * 128:(ic + 1) * 128, :],
                        lambda ic, e_i=e_i: be1s[:, e_i * ICH + ic:e_i * ICH + ic + 1],
                        lambda ic, e_i=e_i: be3s[:, e_i * ICH + ic:e_i * ICH + ic + 1],
                        lambda ti, e_i=e_i: comb[:, ti * E + e_i: ti * E + e_i + 1],
                    )
                  n_sh_units = max(1, SHCH // max(ICH, 1))
                  shw = SHCH // n_sh_units
                  for u in range(n_sh_units):
                    ffn_unit(
                        shw,
                        lambda c, u=u: ws1_b[c * 128:(c + 1) * 128,
                                             u * shw * 128:(u + 1) * shw * 128],
                        lambda c, u=u: ws3_b[c * 128:(c + 1) * 128,
                                             u * shw * 128:(u + 1) * shw * 128],
                        lambda ic, u=u: ws2_b[(u * shw + ic) * 128:
                                              (u * shw + ic + 1) * 128, :],
                        lambda ic, u=u: bs1s[:, u * shw + ic:u * shw + ic + 1],
                        lambda ic, u=u: bs3s[:, u * shw + ic:u * shw + ic + 1],
                        lambda ti: None,
                    )

                  for gt in range(g_tiles):
                    t = g0 + gt
                    ld = e_dma(b, "qxr", xres[:, :], x_ext[t * 128:(t + 1) * 128, :],
                               waits=(("dve", xres_free[0]),))
                    b.op("dve", lambda e, gt=gt: e.tensor_tensor(
                        y_acc[gt][:, :], y_acc[gt][:, :], xres[:, :], AluOpType.add),
                        waits=(("qxr", ld),), inc="dve")
                    xres_free[0] = b.cnt["dve"]
                    c_d = b.cnt["dve"]
                    e_dma(b, "qst", x1_scr[t * 128:(t + 1) * 128, :], y_acc[gt][:, :],
                          waits=(("dve", c_d),))
                    if mode == "l0" and t * 128 >= HALO:
                        r0 = t * 128 - HALO
                        e_dma(b, "qst", out_x1[r0:r0 + 128, :], y_acc[gt][:, :])
                b.barrier()
            l0s.close()

        if do_l1:
            with contextlib.ExitStack() as p4:
                ln = make_ln_sb(p4, IB, pre="p4")
                for (bs, bw) in ext_blocks:
                    emit_ln(ln, lambda t, bs=bs: x1_src[bs + t * 128:bs + (t + 1) * 128, :],
                            bw // 128, xn_ext, bs)
                b.barrier()

            av1 = [sbt(b.es, f"av1_{p}", [128, OWN], BF16) for p in range(NPAIR)]
            p56 = contextlib.ExitStack()
            q1 = [sbt(p56, f"q1_{p}", [128, OWN], BF16) for p in range(NPAIR)]
            k1 = [sbt(p56, f"k1_{p}", [128, EXT], BF16) for p in range(NPAIR)]
            v1 = [sbt(p56, f"v1_{t}", [128, D], BF16) for t in range(n_ext_tiles)]

            with contextlib.ExitStack() as p5:
                ipT = [sbt(p5, f"ipT{c}", [128, D], BF16) for c in range(DCH)]
                qk_tm = sbt(p5, "qk_tm", [128, D], BF16)
                ro1 = sbt(p5, "ro1", [128, D], BF16)
                rc = sbt(p5, "rc", [128, D], BF16)
                rs = sbt(p5, "rs", [128, D], BF16)
                rope_free = [0]

                def qkv_pass(which):
                    b.barrier()
                    prep_T(lambda oc, which=which:
                           w_in[which * D + oc * 128:which * D + (oc + 1) * 128, :],
                           DCH, ipT, stg, stg_bf, IB, D)
                    b.barrier()
                    tiles = range(n_own_tiles) if which == 0 else range(n_ext_tiles)
                    for t in tiles:
                        e0 = t * 128 + (HALO if which == 0 else 0)
                        for half in range(NB):
                            bank = half % 2
                            po = pf(bank, OW)
                            for c in range(DCH):
                                e_mm(b, po, xn_ext[c][:, e0:e0 + 128],
                                     ipT[c][:, half * OW:(half + 1) * OW],
                                     start=(c == 0), stop=(c == DCH - 1),
                                     waits=b.bank_wait(bank) if c == 0 else (),
                                     inc="pe" if c == DCH - 1 else None)
                            c_pe = b.cnt["pe"]
                            if which == 2:
                                b.op("dve", lambda e, po=po, t=t, half=half:
                                     e.tensor_copy(
                                         v1[t][:, half * OW:(half + 1) * OW], po),
                                     waits=(("pe", c_pe),), inc="dve")
                            else:
                                b.op("dve", lambda e, po=po, half=half: e.tensor_copy(
                                    qk_tm[:, half * OW:(half + 1) * OW], po),
                                    waits=(("pe", c_pe), ("pe", rope_free[0])),
                                    inc="dve")
                            b.bank_done(bank, "dve")
                        if which == 2:
                            continue
                        cos_t = ropeq_cos if which == 0 else ropek_cos
                        sin_t = ropeq_sin if which == 0 else ropek_sin
                        l1_ = e_dma(b, "qrope", rc[:, :], cos_t[e0:e0 + 128, :],
                                    waits=(("dve", b.cnt["dve"]),))
                        l2_ = e_dma(b, "qrope", rs[:, :], sin_t[e0:e0 + 128, :])
                        b.op("dve", lambda e: e.tensor_tensor(
                            ro1[:, :], qk_tm[:, :], rc[:, :], AluOpType.mult),
                            waits=(("qrope", l2_),))
                        swp = qk_tm[:, :].rearrange("p (a two) -> p a two", two=2)[:, :, ::-1]
                        b.op("dve", lambda e, swp=swp: e.tensor_tensor(
                            rc[:, :].rearrange("p (a two) -> p a two", two=2), swp,
                            rs[:, :].rearrange("p (a two) -> p a two", two=2),
                            AluOpType.mult))
                        b.op("dve", lambda e: e.tensor_tensor(
                            ro1[:, :], ro1[:, :], rc[:, :], AluOpType.add), inc="dve")
                        c_ro = b.cnt["dve"]
                        dst = q1 if which == 0 else k1
                        for c in range(DCH):
                            halfb = c % 2
                            pslot = pbf(halfb)[:, 0:128]
                            tr = e_tr(b, pslot, ro1[:, c * 128:(c + 1) * 128], IB,
                                      waits=(("dve", c_ro),) + b.bank_wait(("bf", halfb)),
                                      inc="pe")
                            b.op("dve", lambda e, p=pslot, c=c, t=t, dst=dst:
                                 e.tensor_copy(dst[c][:, t * 128:(t + 1) * 128], p),
                                 waits=(("pe", tr),), inc="dve")
                            b.bankg[("bf", halfb)] = (("dve", b.cnt["dve"]),)
                        rope_free[0] = b.cnt["pe"]

                qkv_pass(1)
                qkv_pass(0)
                qkv_pass(2)
                b.barrier()
            tap("xn1", xn_ext)
            tap("q1", q1)
            tap("k1", k1)
            tap("v1", v1)

            with contextlib.ExitStack() as p6:
                msk = sbt(p6, "msk", [128, NKC * 128])
                Ps = sbt(p6, "Ps", [128, NKC * 128])
                Pn = sbt(p6, "Pn", [128, NKC * 128], BF16)
                PtT = sbt(p6, "PtT", [128, NKC * 128], BF16)
                nm1 = sbt(p6, "nm1", [128, 1])
                l1c = sbt(p6, "l1c", [128, 1])
                rl1 = sbt(p6, "rl1", [128, 1])
                for qt in range(n_own_tiles):
                    ldm = e_dma(b, "qmsk", msk[:, :], masks[qt],
                                waits=(("dve", b.cnt["dve"]),))
                    for h in range(NH):
                        pair, rsel = h // 2, h % 2
                        bank = h % 2
                        ps_ = pf(bank, NKC * 128)
                        e_mm(b, ps_, q1[pair][rsel * 64:(rsel + 1) * 64,
                                              qt * 128:(qt + 1) * 128],
                             k1[pair][rsel * 64:(rsel + 1) * 64,
                                      qt * 128:qt * 128 + NKC * 128],
                             start=True, stop=True,
                             waits=(("qmsk", ldm),) + b.bank_wait(bank), inc="pe")
                        c_pe = b.cnt["pe"]
                        b.op("dve", lambda e, ps_=ps_: e.tensor_tensor(
                            ps_, ps_, msk[:, :], AluOpType.add),
                            waits=(("pe", c_pe),), inc="dve")
                        b.op("dve", lambda e, ps_=ps_: e.reduce_max(
                            nm1[:, :], ps_, axis=AX.X, negate=True), inc="dve")
                        c_nm = b.cnt["dve"]
                        b.op("act", lambda e, ps_=ps_: e.activation(
                            Ps[:, :], ps_, AF.Exp, bias=nm1[:, :],
                            accum_out=l1c[:, :]),
                            waits=(("dve", c_nm),), inc="act")
                        c_ex = b.cnt["act"]
                        b.bank_done(bank, "act")
                        b.op("dve", lambda e: e.reciprocal(rl1[:, :], l1c[:, :]),
                             waits=(("act", c_ex),))
                        b.op("dve", lambda e: e.tensor_scalar(
                            Pn[:, :], Ps[:, :], rl1[:, :], None, AluOpType.mult),
                            inc="dve")
                        c_pn = b.cnt["dve"]
                        for i in range(NKC):
                            halfb = i % 2
                            pslot = pbf(halfb)[:, 0:128]
                            tr = e_tr(b, pslot, Pn[:, i * 128:(i + 1) * 128], IB,
                                      waits=(("dve", c_pn),) + b.bank_wait(("bf", halfb)),
                                      inc="pe")
                            b.op("dve", lambda e, p=pslot, i=i: e.tensor_copy(
                                PtT[:, i * 128:(i + 1) * 128], p),
                                waits=(("pe", tr),), inc="dve")
                            b.bankg[("bf", halfb)] = (("dve", b.cnt["dve"]),)
                        c_pt = b.cnt["dve"]
                        avb = pf(4, 128)[0:64, 0:128]
                        for i in range(NKC):
                            e_mm(b, avb, v1[qt + i][:, h * HD:(h + 1) * HD],
                                 PtT[:, i * 128:(i + 1) * 128],
                                 start=(i == 0), stop=(i == NKC - 1),
                                 waits=((("dve", c_pt),) + b.bank_wait(4))
                                 if i == 0 else (),
                                 inc="pe" if i == NKC - 1 else None)
                        c_avp = b.cnt["pe"]
                        b.op("dve", lambda e, pair=pair, rsel=rsel, qt=qt, avb=avb:
                             e.tensor_copy(av1[pair][rsel * 64:(rsel + 1) * 64,
                                                     qt * 128:(qt + 1) * 128], avb),
                             waits=(("pe", c_avp),), inc="dve")
                        b.bank_done(4, "dve")
                b.barrier()
            p56.close()
            tap("av1", av1)

            with contextlib.ExitStack() as p7:
                woT = [sbt(p7, f"woT{c}", [128, D], BF16) for c in range(DCH)]
                prep_T(lambda oc: w_out[oc * 128:(oc + 1) * 128, :], DCH, woT,
                       stg, stg_bf, IB, D)
                tap("woT", woT)
                alpha = sbt(p7, "alpha", [128, 1])
                alph_p = sbt(p7, "alph_p", [128, 1])
                wstage = sbt(p7, "wstage", [128, 128], BF16)

                for hc in range(H1CH):
                    ld = e_dma(b, "qc", stg[:, 0:D], w1[hc * 128:(hc + 1) * 128, :],
                               waits=(("act", b.cnt["act"]), ("qst", b.cnt["qst"]),
                                      ("dve", b.cnt["dve"])))
                    b.op("dve", lambda e: e.tensor_reduce(
                        alpha[:, :], stg[:, 0:D], axis=AX.X, op=AluOpType.add,
                        apply_absolute_value=True), waits=(("qc", ld),))
                    b.op("dve", lambda e: e.tensor_scalar(
                        alpha[:, :], alpha[:, :], 1.0 / D, None, AluOpType.mult),
                        inc="dve")
                    c_al = b.cnt["dve"]
                    b.op("act", lambda e: e.activation(
                        stg_bf[:, 0:D], stg[:, 0:D], AF.Sign),
                        waits=(("dve", c_al), ("pe", b.cnt["pe"])), inc="act")
                    c_sg = b.cnt["act"]
                    cb = b.op("dve", lambda e: e.tensor_scalar(
                        stg_bf[:, 0:D], stg_bf[:, 0:D], alpha[:, :], None,
                        AluOpType.mult), inc="dve", waits=(("act", c_sg),))
                    for dc in range(DCH):
                        halfb = dc % 2
                        pslot = pbf(halfb)[:, 0:128]
                        tr = e_tr(b, pslot, stg_bf[:, dc * 128:(dc + 1) * 128], IB,
                                  waits=(("dve", cb), ("qst", b.cnt["qst"]))
                                  + b.bank_wait(("bf", halfb)), inc="pe")
                        cw = b.op("dve", lambda e, p=pslot: e.tensor_copy(
                            wstage[:, :], p),
                            waits=(("pe", tr), ("qst", b.cnt["qst"])), inc="dve")
                        b.bankg[("bf", halfb)] = (("dve", b.cnt["dve"]),)
                        e_dma(b, "qst", w1t_scr[dc, :, hc * 128:(hc + 1) * 128],
                              wstage[:, :], waits=(("dve", cw),))

                H1C2 = max(1, H1 // 2048)
                for oc in range(DCH):
                    for j in range(H1C2):
                        cw_ = min(2048, H1 - j * 2048)
                        ld = e_dma(b, "qc", stg[:, 0:cw_],
                                   w2[oc * 128:(oc + 1) * 128, j * 2048:j * 2048 + cw_],
                                   waits=(("act", b.cnt["act"]), ("qst", b.cnt["qst"]),
                                          ("dve", b.cnt["dve"])))
                        b.op("dve", lambda e, cw_=cw_: e.tensor_reduce(
                            alph_p[:, :], stg[:, 0:cw_], axis=AX.X, op=AluOpType.add,
                            apply_absolute_value=True), waits=(("qc", ld),))
                        if j == 0:
                            b.op("dve", lambda e: e.tensor_copy(alpha[:, :], alph_p[:, :]))
                        else:
                            b.op("dve", lambda e: e.tensor_tensor(
                                alpha[:, :], alpha[:, :], alph_p[:, :], AluOpType.add))
                        b.op("act", lambda e, j=j, cw_=cw_: e.activation(
                            stg_bf[:, j * 2048:j * 2048 + cw_], stg[:, 0:cw_], AF.Sign),
                            inc="act",
                            waits=(("qc", ld),) + ((("pe", b.cnt["pe"]),) if j == 0 else ()))
                    b.op("dve", lambda e: e.tensor_scalar(
                        alpha[:, :], alpha[:, :], 1.0 / H1, None, AluOpType.mult),
                        inc="dve")
                    c_al = b.cnt["dve"]
                    cb = b.op("dve", lambda e: e.tensor_scalar(
                        stg_bf[:, 0:H1], stg_bf[:, 0:H1], alpha[:, :], None,
                        AluOpType.mult), inc="dve",
                        waits=(("act", b.cnt["act"]), ("dve", c_al)))
                    for hc in range(H1CH):
                        halfb = hc % 2
                        pslot = pbf(halfb)[:, 0:128]
                        tr = e_tr(b, pslot, stg_bf[:, hc * 128:(hc + 1) * 128], IB,
                                  waits=(("dve", cb), ("qst", b.cnt["qst"]))
                                  + b.bank_wait(("bf", halfb)), inc="pe")
                        cw = b.op("dve", lambda e, p=pslot: e.tensor_copy(
                            wstage[:, :], p),
                            waits=(("pe", tr), ("qst", b.cnt["qst"])), inc="dve")
                        b.bankg[("bf", halfb)] = (("dve", b.cnt["dve"]),)
                        e_dma(b, "qst", w2t_scr[hc, :, oc * 128:(oc + 1) * 128],
                              wstage[:, :], waits=(("dve", cw),))
                b.barrier()

                b1s = sbt(p7, "b1s", [128, H1CH])
                for hc in range(H1CH):
                    e_dma(b, "qc", b1s[:, hc:hc + 1], b1_col[hc],
                          waits=(("dve", b.cnt["dve"]),) if hc == 0 else ())
                b2r = sbt(p7, "b2r", [1, D], BF16)
                ldb2 = e_dma(b, "qc", stg[0:1, 0:D], b2_row,
                             waits=(("dve", b.cnt["dve"]), ("act", b.cnt["act"])))
                b.op("dve", lambda e: e.tensor_copy(b2r[:, :], stg[0:1, 0:D]),
                     waits=(("qc", ldb2),), inc="dve")
                h1a = [sbt(p7, f"h1a{i}", [128, 512], BF16) for i in range(H1CH)]
                w1tl = [[sbt(p7, f"w1tl{i}_{c}", [128, 128], BF16) for c in range(DCH)]
                        for i in range(2)]
                w2tl = [sbt(p7, f"w2tl{i}", [128, OW], BF16) for i in range(2)]
                xr1 = sbt(p7, "xr1", [128, D])
                ot = sbt(p7, "ot", [128, D])
                h1a_free = [0]
                w1tl_free = [0, 0]
                w2tl_free = [0, 0]
                xr1_free = [0]
                ot_free = [0]
                c_b1 = b.cnt["qc"]

                for (bs, bw) in own_blocks:
                    nt = bw // 128
                    for hc in range(H1CH):
                        wi = hc % 2
                        qwname = f"qwa{wi}"
                        for dc in range(DCH):
                            e_dma(b, qwname, w1tl[wi][dc][:, :],
                                  w1t_scr[dc, :, hc * 128:(hc + 1) * 128],
                                  waits=(("pe", w1tl_free[wi]),) if dc == 0 else ())
                        c_w = b.cnt[qwname]
                        bank = 4 + (hc % 2)
                        pb = pf(bank, bw)
                        for dc in range(DCH):
                            e_mm(b, pb, w1tl[wi][dc][:, :],
                                 xn_ext[dc][:, HALO + bs:HALO + bs + bw],
                                 start=(dc == 0), stop=(dc == DCH - 1),
                                 waits=(((f"qwa{wi}", c_w),) + b.bank_wait(bank))
                                 if dc == 0 else (),
                                 inc="pe" if dc == DCH - 1 else None)
                        w1tl_free[wi] = b.cnt["pe"]
                        c_pe = b.cnt["pe"]
                        b.op("act", lambda e, pb=pb, hc=hc, bw=bw: e.activation(
                            h1a[hc][:, 0:bw], pb, AF.Gelu, bias=b1s[:, hc:hc + 1]),
                            waits=(("pe", c_pe), ("qc", c_b1), ("pe", h1a_free[0])),
                            inc="act")
                        b.bank_done(bank, "act")
                    c_h1 = b.cnt["act"]
                    for half in range(NB):
                        for t in range(nt):
                            bank = t % 4
                            po = pf(bank, OW)
                            for p in range(NPAIR):
                                e_mm(b, po, av1[p][:, bs + t * 128:bs + (t + 1) * 128],
                                     woT[p][:, half * OW:(half + 1) * OW],
                                     start=(p == 0), stop=False,
                                     waits=b.bank_wait(bank) if p == 0 else (),
                                     inc="pe" if p == NPAIR - 1 else None)
                        for hc in range(0 if "skip_ffn1" in TAPS else H1CH):
                            wi = hc % 2
                            ldw = e_dma(b, f"qwb{wi}", w2tl[wi][:, :],
                                        w2t_scr[hc, :, half * OW:(half + 1) * OW],
                                        waits=(("pe", w2tl_free[wi]),))
                            for t in range(nt):
                                bank = t % 4
                                po = pf(bank, OW)
                                e_mm(b, po, h1a[hc][:, t * 128:(t + 1) * 128],
                                     w2tl[wi][:, :], start=False, stop=False,
                                     waits=((f"qwb{wi}", ldw), ("act", c_h1))
                                     if t == 0 else (),
                                     inc="pe" if t == nt - 1 else None)
                            w2tl_free[wi] = b.cnt["pe"]
                        for t in range(nt):
                            bank = t % 4
                            po = pf(bank, OW)
                            e_mm(b, po, ones_row_bf[:, :],
                                 b2r[:, half * OW:(half + 1) * OW],
                                 start=False, stop=True, inc="pe")
                            c_pe = b.cnt["pe"]
                            ldx = e_dma(b, "qxr", xr1[:, half * OW:(half + 1) * OW],
                                        x1_src[HALO + bs + t * 128:
                                               HALO + bs + (t + 1) * 128,
                                               half * OW:(half + 1) * OW],
                                        waits=(("dve", xr1_free[0]),))
                            b.op("dve", lambda e, po=po, half=half: e.tensor_tensor(
                                ot[:, half * OW:(half + 1) * OW], po,
                                xr1[:, half * OW:(half + 1) * OW], AluOpType.add),
                                waits=(("pe", c_pe), ("qxr", ldx),
                                       ("qout", ot_free[0])), inc="dve")
                            xr1_free[0] = b.cnt["dve"]
                            b.bank_done(bank, "dve")
                            e_dma(b, "qout", out_fin[bs + t * 128:bs + (t + 1) * 128,
                                                     half * OW:(half + 1) * OW],
                                  ot[:, half * OW:(half + 1) * OW],
                                  waits=(("dve", b.cnt["dve"]),))
                            ot_free[0] = b.cnt["qout"]
                    h1a_free[0] = b.cnt["pe"]
        b.barrier()

        with nc.Block() as block:
            b.finalize(block)
    return nc, b



def make_rope_tables(cfg, g0):
    D = cfg["D"]; HD = cfg["HD"]; NH = cfg["NH"]
    EXT = cfg["OWN"] + cfg["HALO"]
    ds = np.arange(HD // 2, dtype=np.float64)
    freqs = np.exp(ds * (-np.log(10000.0) * 2.0 / HD))
    ts = np.arange(g0, g0 + EXT, dtype=np.float64)
    ang = ts[:, None] * freqs[None, :]
    cos = np.cos(ang)
    sin = np.sin(ang)
    cosdup = np.repeat(cos, 2, axis=1)
    sindup = np.empty((EXT, HD))
    sindup[:, 0::2] = -sin
    sindup[:, 1::2] = sin
    cosd = np.tile(cosdup, (1, NH))
    sind = np.tile(sindup, (1, NH))
    return cosd, sind


def make_masks(cfg, g0):
    OWN = cfg["OWN"]; HALO = cfg["HALO"]; CTX = cfg["CTX"]
    NKC = HALO // 128 + 1
    nt = OWN // 128
    out = np.full((nt, 128, NKC * 128), NEG, dtype=np.float32)
    for qt in range(nt):
        gq = g0 + qt * 128 + np.arange(128)
        gk = g0 - HALO + qt * 128 + np.arange(NKC * 128)
        delta = gq[:, None] - gk[None, :]
        ok = (delta >= 0) & (delta < CTX) & (gk[None, :] >= 0)
        out[qt][ok] = 0.0
    return out


def host_inputs_l0(cfg, x_batch, x_ext_np, W):
    D = cfg["D"]; NH = cfg["NH"]; HD = cfg["HD"]
    NPAIR = NH // 2
    ICH = cfg["INTER"] // 128
    SHCH = cfg["SH_I"] // 128
    E = cfg["E"]
    d = dict(
        ident_bf=np.eye(128, dtype=np.float32).astype(BF),
        ones_row_bf=np.ones((1, 128), np.float32).astype(BF),
        ones_row_f=np.ones((1, 128), np.float32),
        x_full=np.ascontiguousarray(x_batch, np.float32),
        x_ext=np.ascontiguousarray(x_ext_np, np.float32),
        wqkv=W["attn_Wqkv"],
        bq8_col=np.ascontiguousarray(
            W["attn_bqkv"][0:NH].reshape(NPAIR, 128, 1), np.float32),
        bk_col=np.ascontiguousarray(
            W["attn_bqkv"][NH:2 * NH].reshape(NPAIR, 128, 1), np.float32),
        bv_row=np.ascontiguousarray(
            W["attn_bqkv"][2 * NH:3 * NH].reshape(1, NH * HD), np.float32),
        wp=W["attn_Wp"],
        bp_row=W["attn_bp"].reshape(1, D),
        gate_w=W["ffn_gate_w"],
        gb_rep=np.ascontiguousarray(
            np.tile(W["ffn_gate_b"].reshape(1, E), (128, 1)), np.float32),
        we1=W["ffn_We1"], we3=W["ffn_We3"], we2=W["ffn_We2"],
        be1_col=np.ascontiguousarray(
            W["ffn_be1"].reshape(E, ICH, 128, 1), np.float32),
        be3_col=np.ascontiguousarray(
            W["ffn_be3"].reshape(E, ICH, 128, 1), np.float32),
        be2=W["ffn_be2"],
        ws1=W["ffn_ws1"], ws3=W["ffn_ws3"], ws2=W["ffn_ws2"],
        bs1_col=np.ascontiguousarray(W["ffn_bs1"].reshape(SHCH, 128, 1), np.float32),
        bs3_col=np.ascontiguousarray(W["ffn_bs3"].reshape(SHCH, 128, 1), np.float32),
        bs2_row=W["ffn_bs2"].reshape(1, D),
    )
    return {k: np.ascontiguousarray(v) for k, v in d.items()}


def host_inputs_l1(cfg, W, g0_own):
    D = cfg["D"]
    H1CH = cfg["H1"] // 128
    qc, qs = make_rope_tables(cfg, g0_own - cfg["HALO"])
    d = dict(
        ident_bf=np.eye(128, dtype=np.float32).astype(BF),
        ones_row_bf=np.ones((1, 128), np.float32).astype(BF),
        ones_row_f=np.ones((1, 128), np.float32),
        in_proj_w=W["attn_in_proj_w"],
        out_proj_w=W["attn_out_proj_w"],
        ropeq_cos=np_bf(qc * (cfg["HD"] ** -0.5)),
        ropeq_sin=np_bf(qs * (cfg["HD"] ** -0.5)),
        ropek_cos=np_bf(qc),
        ropek_sin=np_bf(qs),
        masks=make_masks(cfg, g0_own),
        w1=W["ffn_w1"],
        b1_col=np.ascontiguousarray(W["ffn_b1"].reshape(H1CH, 128, 1), np.float32),
        w2=W["ffn_w2"],
        b2_row=W["ffn_b2"].reshape(1, D),
    )
    return {k: np.ascontiguousarray(v) for k, v in d.items()}


def kernel(**inputs):
    cfg = real_cfg()
    D = cfg["D"]; L = cfg["L"]; OWN = cfg["OWN"]; HALO = cfg["HALO"]
    B = 2
    n_slice = 4
    W0 = {k[3:]: np.asarray(v, np.float32) for k, v in inputs.items()
          if k.startswith("L0_")}
    W1 = {k[3:]: np.asarray(v, np.float32) for k, v in inputs.items()
          if k.startswith("L1_")}
    x = np.asarray(inputs["L0_attn_x"], np.float32)

    def ext_slice(arr_seq, s):
        s0 = s * OWN
        out = np.zeros((OWN + HALO, D), np.float32)
        lo = max(0, s0 - HALO)
        out[HALO - (s0 - lo):] = arr_seq[lo:s0 + OWN]
        return out

    nc0, _ = build_program(cfg, "l0")
    in_maps = []
    for core in range(8):
        bb, ss = core // n_slice, core % n_slice
        in_maps.append(host_inputs_l0(cfg, x[bb], ext_slice(x[bb], ss), W0))
    res0 = run_bass_kernel_spmd(nc0, in_maps, core_ids=list(range(8)))
    x1 = np.zeros((B, L, D), np.float32)
    for core in range(8):
        bb, ss = core // n_slice, core % n_slice
        x1[bb, ss * OWN:(ss + 1) * OWN] = res0.results[core]["x1_own"]

    nc1, _ = build_program(cfg, "l1")
    in_maps = []
    for core in range(8):
        bb, ss = core // n_slice, core % n_slice
        m = host_inputs_l1(cfg, W1, ss * OWN)
        m["x1_ext"] = ext_slice(x1[bb], ss)
        in_maps.append(m)
    res1 = run_bass_kernel_spmd(nc1, in_maps, core_ids=list(range(8)))
    out = np.zeros((B, L, D), np.float32)
    for core in range(8):
        bb, ss = core // n_slice, core % n_slice
        out[bb, ss * OWN:(ss + 1) * OWN] = res1.results[core]["out_own"]
    return out


# version 04
# speedup vs baseline: 1.0072x; 1.0072x over previous
import contextlib
from collections import defaultdict

import numpy as np
import ml_dtypes

import concourse.bass as bass
import concourse.mybir as mybir
from concourse.alu_op_type import AluOpType
from concourse.bass_utils import run_bass_kernel_spmd

F32 = mybir.dt.float32
BF16 = mybir.dt.bfloat16
AF = mybir.ActivationFunctionType
AX = mybir.AxisListType
BF = ml_dtypes.bfloat16
NEG = -1.0e30

ENGINES = ("pe", "dve", "act", "pool", "sp")
DETECT_RACES = False
TAPS = set()


def real_cfg():
    return dict(D=1024, NH=16, HD=64, L=8192, OWN=2048, HALO=256,
                E=16, G=4, INTER=512, SH_I=1024, H1=4096, CTX=250)


def tiny_cfg():
    return dict(D=256, NH=4, HD=64, L=1024, OWN=256, HALO=128,
                E=4, G=2, INTER=128, SH_I=256, H1=512, CTX=100)


def mid_cfg():
    return dict(D=256, NH=4, HD=64, L=8192, OWN=2048, HALO=128,
                E=4, G=2, INTER=128, SH_I=256, H1=512, CTX=100)


def blocks_of(total, width=512):
    out = []
    s = 0
    rem = total % width
    if rem:
        out.append((0, rem))
        s = rem
    while s < total:
        out.append((s, width))
        s += width
    return out


class Builder:
    def __init__(self, nc):
        self.nc = nc
        self.em = {e: [] for e in ENGINES}
        self.cnt = defaultdict(int)
        self.sems = {}
        self.es = contextlib.ExitStack()
        self.bankg = {}
        self.dma_sems = set()

    def sem(self, name):
        if name not in self.sems:
            self.sems[name] = self.es.enter_context(self.nc.semaphore(name))
        return self.sems[name]

    def op(self, eng, build_fn, inc=None, incby=1, waits=(), auto=True):
        if auto and eng in ("dve", "act"):
            waits = tuple(waits) + ((eng, self.cnt[eng]),)
            if inc is None:
                inc, incby = eng, 1
        waits = tuple((s, v) for (s, v) in waits if v > 0)
        if inc is not None:
            self.sem(inc)
        for s, _ in waits:
            self.sem(s)
        semd = self.sems

        def f(e):
            for s, v in waits:
                e.wait_ge(semd[s], v)
            ins = build_fn(e)
            if inc is not None and ins is not None:
                ins.then_inc(semd[inc], incby)
        self.em[eng].append(f)
        if inc is not None:
            self.cnt[inc] += incby
        return self.cnt[inc] if inc is not None else None

    def w(self, *names):
        return tuple((n, self.cnt[n]) for n in names)

    def barrier(self):
        on = [s for s in (("pe", "dve", "act", "pool") + tuple(sorted(self.dma_sems)))
              if self.cnt[s] > 0]
        waits = self.w(*on)
        for e in ENGINES:
            self.op(e, lambda _e: None, waits=waits, auto=False)

    def bank_wait(self, *banks):
        out = []
        for bk in banks:
            out.extend(self.bankg.get(bk, ()))
        return tuple(out)

    def bank_done(self, bank, sem):
        self.bankg[bank] = ((sem, self.cnt[sem]),)

    def finalize(self, block):
        eng_objs = {"pe": block.tensor, "dve": block.vector, "act": block.scalar,
                    "pool": block.gpsimd, "sp": block.sync}
        for name in ENGINES:
            if not self.em[name]:
                continue

            def make(fns):
                def body(e):
                    for f in fns:
                        f(e)
                return body
            eng_objs[name](make(self.em[name]))


def e_dma(b, sem, out_ap, in_ap, waits=()):
    b.dma_sems.add(sem)
    return b.op("sp", lambda e: e.dma_start(out=out_ap, in_=in_ap),
                inc=sem, incby=16, waits=waits)


def e_mm(b, out_ap, lhsT, rhs, start, stop, waits=(), inc=None):
    return b.op("pe", lambda e: e.matmul(out_ap, lhsT, rhs, start=start, stop=stop),
                waits=waits, inc=inc)


def e_tr(b, out_ap, in_ap, ident, waits=(), inc=None):
    return b.op("pe", lambda e: e.transpose(out_ap, in_ap, ident),
                waits=waits, inc=inc)


def np_bf(x):
    return np.ascontiguousarray(np.asarray(x, dtype=np.float32)).astype(BF)



def build_program(cfg, mode):
    D = cfg["D"]; NH = cfg["NH"]; HD = cfg["HD"]; L = cfg["L"]
    OWN = cfg["OWN"]; HALO = cfg["HALO"]; E = cfg["E"]; G = cfg["G"]
    INTER = cfg["INTER"]; SH_I = cfg["SH_I"]; H1 = cfg["H1"]
    EXT = OWN + HALO
    DCH = D // 128
    NPAIR = NH // 2
    ICH = INTER // 128
    SHCH = SH_I // 128
    H1CH = H1 // 128
    GS = E // G
    NKC = HALO // 128 + 1
    LKB = L // 512
    LC = L // 128
    ext_blocks = blocks_of(EXT)
    own_blocks = blocks_of(OWN)
    full_blocks = blocks_of(L)
    n_ext_tiles = EXT // 128
    n_own_tiles = OWN // 128
    scale0 = float(HD) ** 0.5
    NB = max(1, D // 512)
    OW = min(512, D)
    SB_N = min(5, LC)

    do_l0 = mode in ("l0", "fused")
    do_l1 = mode in ("l1", "fused")

    nc = bass.Bass("TRN2", target_bir_lowering=False,
                   detect_race_conditions=DETECT_RACES,
                   dynamic_dma_scratch_size=4096)
    b = Builder(nc)

    def din(name, shape, dtype=F32):
        return nc.dram_tensor(name, list(shape), dtype, kind="ExternalInput").ap()

    def dscr(name, shape, dtype=BF16):
        return nc.dram_tensor(name, list(shape), dtype).ap()

    ident_bf_d = din("ident_bf", [128, 128], BF16)
    ones_row_bf_d = din("ones_row_bf", [1, 128], BF16)
    ones_row_f_d = din("ones_row_f", [1, 128], F32)

    if do_l0:
        x_full = din("x_full", [L, D])
        x_ext = din("x_ext", [EXT, D])
        wqkv = din("wqkv", [3 * NH, D, HD])
        bq8_col = din("bq8_col", [NPAIR, 128, 1])
        bk_col = din("bk_col", [NPAIR, 128, 1])
        bv_row = din("bv_row", [1, NH * HD])
        wp = din("wp", [D, D])
        bp_row = din("bp_row", [1, D])
        gate_w = din("gate_w", [E, D])
        gb_rep = din("gb_rep", [128, E])
        we1 = din("we1", [E, D, INTER])
        we3 = din("we3", [E, D, INTER])
        we2 = din("we2", [E, INTER, D])
        be1_col = din("be1_col", [E, ICH, 128, 1])
        be3_col = din("be3_col", [E, ICH, 128, 1])
        be2 = din("be2", [E, D])
        ws1 = din("ws1", [D, SH_I])
        ws3 = din("ws3", [D, SH_I])
        ws2 = din("ws2", [SH_I, D])
        bs1_col = din("bs1_col", [SHCH, 128, 1])
        bs3_col = din("bs3_col", [SHCH, 128, 1])
        bs2_row = din("bs2_row", [1, D])
        wqkv_b = dscr("wqkv_b", [3 * NH, D, HD])
        we1_b = dscr("we1_b", [E, D, INTER])
        we3_b = dscr("we3_b", [E, D, INTER])
        we2_b = dscr("we2_b", [E, INTER, D])
        ws1_b = dscr("ws1_b", [D, SH_I])
        ws3_b = dscr("ws3_b", [D, SH_I])
        ws2_b = dscr("ws2_b", [SH_I, D])
        be2_b = dscr("be2_b", [E, D])
        k_scr = dscr("k_scr", [NH, HD, L])
        v_scr = dscr("v_scr", [NH, LC, 128, HD])
        q_scr = dscr("q_scr", [NH, HD, EXT])
        x1_scr = nc.dram_tensor("x1_scr", [EXT, D], F32).ap()
    if do_l1:
        w_in = din("in_proj_w", [3 * D, D])
        w_out = din("out_proj_w", [D, D])
        ropeq_cos = din("ropeq_cos", [EXT, D], BF16)
        ropeq_sin = din("ropeq_sin", [EXT, D], BF16)
        ropek_cos = din("ropek_cos", [EXT, D], BF16)
        ropek_sin = din("ropek_sin", [EXT, D], BF16)
        masks = din("masks", [n_own_tiles, 128, NKC * 128])
        w1 = din("w1", [H1, D])
        b1_col = din("b1_col", [H1CH, 128, 1])
        w2 = din("w2", [D, H1])
        b2_row = din("b2_row", [1, D])
        w1t_scr = dscr("w1t_scr", [DCH, 128, H1])
        w2t_scr = dscr("w2t_scr", [H1CH, 128, D])
        if mode == "l1":
            x1_src = din("x1_ext", [EXT, D])
        else:
            x1_src = x1_scr

    if mode == "l0":
        out_x1 = nc.dram_tensor("x1_own", [OWN, D], F32, kind="ExternalOutput").ap()
    else:
        out_fin = nc.dram_tensor("out_own", [OWN, D], F32, kind="ExternalOutput").ap()

    def sbt(stack, name, shape, dtype=F32):
        return stack.enter_context(nc.sbuf_tensor(name, list(shape), dtype))

    psum_f = b.es.enter_context(nc.psum_tensor("psum_f", [128, 3072], F32))
    psum_bf = b.es.enter_context(nc.psum_tensor("psum_bf", [128, 2048], BF16))

    def pf(bank, width=512):
        return psum_f[:, bank * 512: bank * 512 + width]

    def pbf(half, width=1024):
        return psum_bf[:, half * 1024: half * 1024 + width]

    def cast_to_scr(src2d, dst2d, rows, cols, stg, stg_bf, colch):
        last_cp = [b.cnt["dve"]]
        last_st = [b.cnt["qcs"]]
        for r0 in range(0, rows, 128):
            rn = min(128, rows - r0)
            for c0 in range(0, cols, colch):
                cn = min(colch, cols - c0)
                ld = e_dma(b, "qc", stg[0:rn, 0:cn], src2d[r0:r0 + rn, c0:c0 + cn],
                           waits=(("dve", last_cp[0]),))
                last_cp[0] = b.op(
                    "dve", lambda e, i=stg[0:rn, 0:cn], o=stg_bf[0:rn, 0:cn]:
                    e.tensor_copy(o, i), inc="dve",
                    waits=(("qc", ld), ("qcs", last_st[0])))
                last_st[0] = e_dma(b, "qcs", dst2d[r0:r0 + rn, c0:c0 + cn],
                                   stg_bf[0:rn, 0:cn], waits=(("dve", last_cp[0]),))

    def emit_ln(s, x_src_fn, n_tiles, dest_chunks, dest_col0, dest_guard=0):
        inv_d = 1.0 / D
        for t in range(n_tiles):
            xt = s["x"][t % 2]
            xn = s["xn"][t % 2]
            ld = e_dma(b, f"lq{t % 2}", xt[:, :], x_src_fn(t),
                       waits=(("dve", s["x_free"][t % 2]),))
            b.op("dve", lambda e, a=xt: e.reduce_sum(s["m"][:, :], a[:, :], axis=AX.X),
                 waits=((f"lq{t % 2}", ld),))
            b.op("dve", lambda e: e.tensor_scalar(s["m"][:, :], s["m"][:, :], inv_d,
                                                  None, AluOpType.mult))
            b.op("dve", lambda e, a=xt: e.tensor_scalar(
                s["xc"][:, :], a[:, :], s["m"][:, :], None, AluOpType.subtract),
                inc="dve")
            s["x_free"][t % 2] = b.cnt["dve"]
            c_xc = b.cnt["dve"]
            b.op("act", lambda e: e.activation(s["xsq"][:, :], s["xc"][:, :], AF.Square,
                                               accum_out=s["ssq"][:, :]),
                 waits=(("dve", c_xc),), inc="act")
            c_sq = b.cnt["act"]
            b.op("dve", lambda e: e.tensor_scalar(
                s["ssq"][:, :], s["ssq"][:, :], inv_d, 1e-5,
                AluOpType.mult, AluOpType.add), waits=(("act", c_sq),), inc="dve")
            c_v = b.cnt["dve"]
            b.op("act", lambda e: e.activation(s["sd"][:, :], s["ssq"][:, :], AF.Sqrt),
                 waits=(("dve", c_v),), inc="act")
            c_sd = b.cnt["act"]
            b.op("dve", lambda e: e.reciprocal(s["rstd"][:, :], s["sd"][:, :]),
                 waits=(("act", c_sd),))
            b.op("dve", lambda e, o=xn: e.tensor_scalar(
                o[:, :], s["xc"][:, :], s["rstd"][:, :], None, AluOpType.mult),
                inc="dve", waits=(("pe", s["xn_free"][t % 2]),))
            c_xn = b.cnt["dve"]
            half = t % 2
            for c in range(DCH):
                pslot = pbf(half)[:, c * 128:(c + 1) * 128]
                e_tr(b, pslot, xn[:, c * 128:(c + 1) * 128], s["IB"],
                     waits=((("dve", c_xn),) + b.bank_wait(("bf", half)))
                     if c == 0 else (),
                     inc="pe" if c == DCH - 1 else None)
            s["xn_free"][t % 2] = b.cnt["pe"]
            c_tr = b.cnt["pe"]
            for c in range(DCH):
                b.op("dve", lambda e, p=pbf(half)[:, c * 128:(c + 1) * 128],
                     o=dest_chunks[c][:, dest_col0 + t * 128:dest_col0 + (t + 1) * 128]:
                     e.tensor_copy(o, p),
                     waits=(("pe", c_tr), ("pe", dest_guard)) if c == 0 else (),
                     inc="dve" if c == DCH - 1 else None)
            b.bankg[("bf", half)] = (("dve", b.cnt["dve"]),)

    def make_ln_sb(stk, IB, pre=""):
        return dict(
            x=[sbt(stk, pre + "lnx0", [128, D]), sbt(stk, pre + "lnx1", [128, D])],
            xc=sbt(stk, pre + "lnxc", [128, D]), xsq=sbt(stk, pre + "lnxsq", [128, D]),
            m=sbt(stk, pre + "lnm", [128, 1]), ssq=sbt(stk, pre + "lnssq", [128, 1]),
            sd=sbt(stk, pre + "lnsd", [128, 1]), rstd=sbt(stk, pre + "lnrstd", [128, 1]),
            xn=[sbt(stk, pre + "lnxn0", [128, D], BF16),
                sbt(stk, pre + "lnxn1", [128, D], BF16)],
            x_free=[0, 0], xn_free=[0, 0], IB=IB,
        )

    def prep_T(src_rows_fn, n_row_chunks, dest_tiles, stg, stg_bf, IB, row_w,
               reader_guard=0):
        first = True
        for oc in range(n_row_chunks):
            ld = e_dma(b, "qc", stg[:, 0:row_w], src_rows_fn(oc),
                       waits=(("dve", b.cnt["dve"]),))
            cp = b.op("dve", lambda e, o=stg_bf[:, 0:row_w], i=stg[:, 0:row_w]:
                      e.tensor_copy(o, i), inc="dve",
                      waits=(("qc", ld), ("pe", b.cnt["pe"]),
                             ("qcs", b.cnt["qcs"]), ("qst", b.cnt["qst"])))
            for ic in range(row_w // 128):
                half = ic % 2
                pslot = pbf(half)[:, 0:128]
                tr = e_tr(b, pslot, stg_bf[:, ic * 128:(ic + 1) * 128], IB,
                          waits=(("dve", cp), ("pe", reader_guard if first else 0))
                          + b.bank_wait(("bf", half)), inc="pe")
                first = False
                b.op("dve", lambda e, p=pslot,
                     o=dest_tiles[ic][:, oc * 128:(oc + 1) * 128]:
                     e.tensor_copy(o, p), waits=(("pe", tr),), inc="dve")
                b.bankg[("bf", half)] = (("dve", b.cnt["dve"]),)

    def tap(name, tiles, per_tile_cols=None):
        if name not in TAPS:
            return
        W_ = tiles[0].shape[1] if per_tile_cols is None else per_tile_cols
        o = nc.dram_tensor(f"tap_{name}", [len(tiles), 128, W_], F32,
                           kind="ExternalOutput").ap()
        b.barrier()
        with nc.sbuf_tensor(f"tap_{name}_s", [128, W_], F32) as tapt:
            for i, t_ in enumerate(tiles):
                c = b.op("dve", lambda e, t_=t_: e.tensor_copy(tapt[:, :], t_[:, 0:W_]),
                         inc="dve", waits=(("qtap", b.cnt["qtap"]),))
                e_dma(b, "qtap", o[i], tapt[:, :], waits=(("dve", c),))
            b.barrier()

    with b.es:
        ident_bf = sbt(b.es, "ident_s", [128, 128], BF16)
        ones_row_bf = sbt(b.es, "ones_bf_s", [1, 128], BF16)
        ones_row_f = sbt(b.es, "ones_f_s", [1, 128], F32)
        for d_ap, s_t in ((ident_bf_d, ident_bf), (ones_row_bf_d, ones_row_bf),
                          (ones_row_f_d, ones_row_f)):
            e_dma(b, "qc", s_t[:, :], d_ap)
        stg = sbt(b.es, "stg", [128, 2048], F32)
        stg_bf = sbt(b.es, "stg_bf", [128, 4096], BF16)
        IB = ident_bf[:, :]

        xn_ext = [sbt(b.es, f"xne{c}", [128, EXT], BF16) for c in range(DCH)]

        if do_l0:
            l0s = contextlib.ExitStack()
            comb = sbt(l0s, "comb", [128, n_ext_tiles * E])
            av_fm = [sbt(l0s, f"avfm{p}", [128, EXT], BF16) for p in range(NPAIR)]
            wpT = [sbt(l0s, f"wpT{c}", [128, D], BF16) for c in range(DCH)]
            gwT = [sbt(l0s, f"gwT{c}", [128, E], BF16) for c in range(DCH)]
            gbr = sbt(l0s, "gbr", [128, E])
            bvr = sbt(l0s, "bvr", [1, NH * HD], BF16)
            bpsr = sbt(l0s, "bpsr", [1, D], BF16)
            bq8c = sbt(l0s, "bq8c", [128, NPAIR])
            bkc = sbt(l0s, "bkc", [128, NPAIR])

            if "skip_casts" not in TAPS:
              cast_to_scr(wqkv.rearrange("e d c -> (e d) c"),
                        wqkv_b.rearrange("e d c -> (e d) c"),
                        3 * NH * D, HD, stg, stg_bf, HD)
              for src, dst, rr, cc in (
                (we1.rearrange("e d i -> (e d) i"), we1_b.rearrange("e d i -> (e d) i"), E * D, INTER),
                (we3.rearrange("e d i -> (e d) i"), we3_b.rearrange("e d i -> (e d) i"), E * D, INTER),
                (we2.rearrange("e i d -> (e i) d"), we2_b.rearrange("e i d -> (e i) d"), E * INTER, D),
                (ws1, ws1_b, D, SH_I), (ws3, ws3_b, D, SH_I), (ws2, ws2_b, SH_I, D),
                (be2, be2_b, E, D),
              ):
                cast_to_scr(src, dst, rr, cc, stg, stg_bf, min(2048, cc))
            if "skip_prept" not in TAPS:
              prep_T(lambda oc: wp[oc * 128:(oc + 1) * 128, :], DCH, wpT,
                   stg, stg_bf, IB, D)
            if "skip_gatebias" not in TAPS:
              _gb_sentinel = 1
            ld = e_dma(b, "qc", stg[0:E, 0:D], gate_w, waits=(("dve", b.cnt["dve"]),))
            cp = b.op("dve", lambda e: e.tensor_copy(stg_bf[0:E, 0:D], stg[0:E, 0:D]),
                      inc="dve", waits=(("qc", ld), ("qcs", b.cnt["qcs"])))
            for c in range(DCH):
                half = c % 2
                pslot = pbf(half)[:, 0:E]
                tr = e_tr(b, pslot, stg_bf[0:E, c * 128:(c + 1) * 128],
                          ident_bf[0:E, 0:E],
                          waits=(("dve", cp),) + b.bank_wait(("bf", half)), inc="pe")
                b.op("dve", lambda e, p=pslot, c=c: e.tensor_copy(gwT[c][:, :], p),
                     waits=(("pe", tr),), inc="dve")
                b.bankg[("bf", half)] = (("dve", b.cnt["dve"]),)
            e_dma(b, "qc", gbr[:, :], gb_rep)
            ldv = e_dma(b, "qc", stg[0:1, 0:NH * HD], bv_row,
                        waits=(("dve", b.cnt["dve"]),))
            b.op("dve", lambda e: e.tensor_copy(bvr[:, :], stg[0:1, 0:NH * HD]),
                 inc="dve", waits=(("qc", ldv),))
            ld1 = e_dma(b, "qc", stg[0:1, 0:D], bp_row, waits=(("dve", b.cnt["dve"]),))
            ld2 = e_dma(b, "qc", stg[0:1, D:2 * D], bs2_row)
            b.op("dve", lambda e: e.tensor_tensor(stg[0:1, 0:D], stg[0:1, 0:D],
                                                  stg[0:1, D:2 * D], AluOpType.add),
                 waits=(("qc", max(ld1, ld2)),))
            b.op("dve", lambda e: e.tensor_copy(bpsr[:, :], stg[0:1, 0:D]), inc="dve")
            for p in range(NPAIR):
                e_dma(b, "qc", bq8c[:, p:p + 1], bq8_col[p])
                e_dma(b, "qc", bkc[:, p:p + 1], bk_col[p])
            b.barrier()

            if "skip_p1" in TAPS:
                pass
            elif True:
              with contextlib.ExitStack() as p1:
                wq_s = [[sbt(p1, f"wq{p}_{c}", [128, 128], BF16) for c in range(DCH)]
                        for p in range(NPAIR)]
                wk_s = [[sbt(p1, f"wk{p}_{c}", [128, 128], BF16) for c in range(DCH)]
                        for p in range(NPAIR)]
                n_vg = max(1, (NH * HD) // 512)
                vgw = (NH * HD) // n_vg
                hpg = vgw // HD
                wv_s = [[sbt(p1, f"wv{g}_{c}", [128, vgw], BF16) for c in range(DCH)]
                        for g in range(n_vg)]
                for p in range(NPAIR):
                    for c in range(DCH):
                        e_dma(b, "qc", wq_s[p][c][:, :].rearrange(
                                  "d (e c) -> d e c", e=2),
                              wqkv_b[2 * p:2 * p + 2, c * 128:(c + 1) * 128, :]
                              .rearrange("e d c -> d e c"))
                        e_dma(b, "qc", wk_s[p][c][:, :].rearrange(
                                  "d (e c) -> d e c", e=2),
                              wqkv_b[NH + 2 * p:NH + 2 * p + 2,
                                     c * 128:(c + 1) * 128, :]
                              .rearrange("e d c -> d e c"))
                for g in range(n_vg):
                    for c in range(DCH):
                        e_dma(b, "qc", wv_s[g][c][:, :].rearrange(
                                  "d (e c) -> d e c", c=HD),
                              wqkv_b[2 * NH + g * hpg:2 * NH + (g + 1) * hpg,
                                     c * 128:(c + 1) * 128, :]
                              .rearrange("e d c -> d e c"))
                ln = make_ln_sb(p1, IB)
                xn_blk = [sbt(p1, f"xnb{c}", [128, 512], BF16) for c in range(DCH)]
                kvs = [sbt(p1, f"kvs{i}", [128, 512], BF16) for i in range(2)]
                kvs_free = [0, 0]
                xnblk_guard = [0]
                b.barrier()

                def kqv_block(bs, bw, dest, dest_c0, is_q):
                    c_ln = b.cnt["dve"]
                    for p in range(NPAIR):
                        bank = p % 4
                        pb = pf(bank, bw)
                        ws = wq_s if is_q else wk_s
                        for c in range(DCH):
                            e_mm(b, pb, ws[p][c][:, :], dest[c][:, dest_c0:dest_c0 + bw],
                                 start=(c == 0), stop=(c == DCH - 1),
                                 waits=((("dve", c_ln),) + b.bank_wait(bank))
                                 if c == 0 else (),
                                 inc="pe" if c == DCH - 1 else None)
                        c_pe = b.cnt["pe"]
                        ki = p % 2
                        st = kvs[ki]
                        for hh in range(2):
                            if is_q:
                                b.op("dve", lambda e, src=pb[hh * 64:(hh + 1) * 64, 0:bw],
                                     o=st[hh * 64:(hh + 1) * 64, 0:bw],
                                     s1=bq8c[hh * 64:(hh + 1) * 64, p:p + 1]:
                                     e.tensor_scalar(o, src, s1, scale0,
                                                     AluOpType.add, AluOpType.mult),
                                     inc="dve",
                                     waits=(("pe", c_pe), (f"kst{ki}", kvs_free[ki]))
                                     if hh == 0 else ())
                            else:
                                b.op("dve", lambda e, src=pb[hh * 64:(hh + 1) * 64, 0:bw],
                                     o=st[hh * 64:(hh + 1) * 64, 0:bw],
                                     s1=bkc[hh * 64:(hh + 1) * 64, p:p + 1]:
                                     e.tensor_scalar(o, src, s1, None, AluOpType.add),
                                     inc="dve",
                                     waits=(("pe", c_pe), (f"kst{ki}", kvs_free[ki]))
                                     if hh == 0 else ())
                        b.bank_done(bank, "dve")
                        c_ev = b.cnt["dve"]
                        scr = q_scr if is_q else k_scr
                        for hh in range(2):
                            e_dma(b, f"kst{ki}", scr[2 * p + hh, :, bs:bs + bw],
                                  st[hh * 64:(hh + 1) * 64, 0:bw],
                                  waits=(("dve", c_ev),) if hh == 0 else ())
                        kvs_free[ki] = b.cnt[f"kst{ki}"]

                def v_block(bs, bw):
                    c_ln = b.cnt["dve"]
                    nt = bw // 128
                    for t in range(nt):
                        for g in range(n_vg):
                            bank = (t * n_vg + g) % 4
                            pb = pf(bank, vgw)
                            for c in range(DCH):
                                e_mm(b, pb, xn_blk[c][:, t * 128:(t + 1) * 128],
                                     wv_s[g][c][:, :], start=(c == 0), stop=False,
                                     waits=((("dve", c_ln),) + b.bank_wait(bank))
                                     if c == 0 else ())
                            e_mm(b, pb, ones_row_bf[:, :], bvr[:, g * vgw:(g + 1) * vgw],
                                 start=False, stop=True, inc="pe")
                            c_pe = b.cnt["pe"]
                            ki = (t * n_vg + g) % 2
                            st = kvs[ki]
                            b.op("dve", lambda e, pb=pb, o=st[:, 0:vgw]:
                                 e.tensor_copy(o, pb), inc="dve",
                                 waits=(("pe", c_pe), (f"kst{ki}", kvs_free[ki])))
                            b.bank_done(bank, "dve")
                            c_ev = b.cnt["dve"]
                            ci = (bs + t * 128) // 128
                            for hh in range(hpg):
                                e_dma(b, f"kst{ki}", v_scr[g * hpg + hh, ci, :, :],
                                      st[:, hh * HD:(hh + 1) * HD],
                                      waits=(("dve", c_ev),) if hh == 0 else ())
                            kvs_free[ki] = b.cnt[f"kst{ki}"]

                for (bs, bw) in full_blocks:
                    emit_ln(ln, lambda t, bs=bs: x_full[bs + t * 128:bs + (t + 1) * 128, :],
                            bw // 128, xn_blk, 0, dest_guard=xnblk_guard[0])
                    kqv_block(bs, bw, xn_blk, 0, is_q=False)
                    v_block(bs, bw)
                    xnblk_guard[0] = b.cnt["pe"]

                gones = sbt(p1, "gones", [128, max(GS, E)])
                b.op("dve", lambda e: e.memset(gones[:, :], 1.0), inc="dve")
                gs_ = dict(
                    p=sbt(p1, "g_p", [128, E]), nm=sbt(p1, "g_nm", [128, 1]),
                    l=sbt(p1, "g_l", [128, 1]), rl=sbt(p1, "g_rl", [128, 1]),
                    sb_=sbt(p1, "g_sb", [128, E]),
                    pair=sbt(p1, "g_pair", [128, G * (GS * (GS - 1) // 2)]),
                    gsc=sbt(p1, "g_gsc", [128, G]), cnt=sbt(p1, "g_cnt", [128, G]),
                    t=sbt(p1, "g_t", [128, E]), keep=sbt(p1, "g_keep", [128, G]),
                    k16=sbt(p1, "g_k16", [128, E]), m1=sbt(p1, "g_m1", [128, 1]),
                    m2=sbt(p1, "g_m2", [128, 1]), msb=sbt(p1, "g_msb", [128, E]),
                )

                def gate_tile(ti):
                    s = gs_
                    pg = pf(5, E)
                    c0 = ti * 128
                    for c in range(DCH):
                        e_mm(b, pg, xn_ext[c][:, c0:c0 + 128], gwT[c][:, :],
                             start=(c == 0), stop=(c == DCH - 1),
                             waits=b.bank_wait(5) if c == 0 else (),
                             inc="pe" if c == DCH - 1 else None)
                    c_pe = b.cnt["pe"]
                    b.op("dve", lambda e: e.reduce_max(s["nm"][:, :], pg, axis=AX.X,
                                                       negate=True),
                         waits=(("pe", c_pe),), inc="dve")
                    b.op("act", lambda e: e.activation(s["p"][:, :], pg, AF.Exp,
                                                       bias=s["nm"][:, :],
                                                       accum_out=s["l"][:, :]),
                         waits=(("dve", b.cnt["dve"]),), inc="act")
                    b.bank_done(5, "act")
                    b.op("dve", lambda e: e.reciprocal(s["rl"][:, :], s["l"][:, :]),
                         waits=(("act", b.cnt["act"]),))
                    b.op("dve", lambda e: e.tensor_scalar(
                        s["p"][:, :], s["p"][:, :], s["rl"][:, :], None, AluOpType.mult))
                    b.op("dve", lambda e: e.tensor_tensor(
                        s["sb_"][:, :], s["p"][:, :], gbr[:, :], AluOpType.add))
                    npr = GS * (GS - 1) // 2
                    sb3 = s["sb_"][:, :].rearrange("p (g s) -> p g s", s=GS)
                    pi = 0
                    for i in range(GS):
                        for j in range(i + 1, GS):
                            b.op("dve", lambda e, i=i, j=j, pi=pi: e.tensor_tensor(
                                s["pair"][:, pi * G:(pi + 1) * G], sb3[:, :, i],
                                sb3[:, :, j], AluOpType.add))
                            pi += 1
                    b.op("dve", lambda e: e.tensor_copy(s["gsc"][:, :], s["pair"][:, 0:G]))
                    for pi in range(1, npr):
                        b.op("dve", lambda e, pi=pi: e.tensor_tensor(
                            s["gsc"][:, :], s["gsc"][:, :],
                            s["pair"][:, pi * G:(pi + 1) * G], AluOpType.max))
                    b.op("dve", lambda e: e.memset(s["cnt"][:, :], 0.0))
                    for gg in range(G):
                        b.op("dve", lambda e, gg=gg: e.tensor_scalar(
                            s["t"][:, 0:G], s["gsc"][:, :], s["gsc"][:, gg:gg + 1],
                            None, AluOpType.is_lt))
                        b.op("dve", lambda e: e.tensor_tensor(
                            s["cnt"][:, :], s["cnt"][:, :], s["t"][:, 0:G],
                            AluOpType.add))
                    b.op("dve", lambda e: e.tensor_scalar(
                        s["keep"][:, :], s["cnt"][:, :], 2.0, None, AluOpType.is_lt))
                    for gg in range(G):
                        b.op("dve", lambda e, gg=gg: e.tensor_scalar(
                            s["k16"][:, gg * GS:(gg + 1) * GS], gones[:, 0:GS],
                            s["keep"][:, gg:gg + 1], None, AluOpType.mult))
                    b.op("dve", lambda e: e.tensor_scalar(
                        s["t"][:, 0:E], s["k16"][:, :], 1.0, -NEG,
                        AluOpType.subtract, AluOpType.mult))
                    b.op("dve", lambda e: e.tensor_tensor(
                        s["msb"][:, :], s["sb_"][:, :], s["t"][:, 0:E], AluOpType.add))
                    b.op("dve", lambda e: e.reduce_max(s["m1"][:, :], s["msb"][:, :],
                                                       axis=AX.X))
                    b.op("dve", lambda e: e.tensor_scalar(
                        s["t"][:, 0:E], s["msb"][:, :], s["m1"][:, :], None,
                        AluOpType.is_ge))
                    b.op("dve", lambda e: e.tensor_scalar(
                        s["t"][:, 0:E], s["t"][:, 0:E], NEG, None, AluOpType.mult))
                    b.op("dve", lambda e: e.tensor_tensor(
                        s["t"][:, 0:E], s["msb"][:, :], s["t"][:, 0:E], AluOpType.add))
                    b.op("dve", lambda e: e.reduce_max(s["m2"][:, :], s["t"][:, 0:E],
                                                       axis=AX.X))
                    b.op("dve", lambda e: e.tensor_scalar(
                        s["t"][:, 0:E], s["msb"][:, :], s["m2"][:, :], None,
                        AluOpType.is_ge))
                    b.op("dve", lambda e, ti=ti: e.tensor_tensor(
                        comb[:, ti * E:(ti + 1) * E], s["p"][:, :], s["t"][:, 0:E],
                        AluOpType.mult), inc="dve")

                for (bs, bw) in ext_blocks:
                    emit_ln(ln, lambda t, bs=bs: x_ext[bs + t * 128:bs + (t + 1) * 128, :],
                            bw // 128, xn_ext, bs)
                    kqv_block(bs, bw, xn_ext, bs, is_q=True)
                    for t in range(bw // 128):
                        gate_tile((bs + t * 128) // 128)
                b.barrier()
            tap("xn0", xn_ext)
            tap("comb", [comb])

            if "skip_p2" in TAPS:
                pass
            elif True:
              with contextlib.ExitStack() as p2:
                kbuf = [sbt(p2, f"kbuf{i}", [65, L], BF16) for i in range(2)]
                vbuf = [sbt(p2, f"vbuf{i}", [128, LC * 65], BF16) for i in range(2)]
                qbuf = [sbt(p2, f"qbuf{i}", [65, EXT], BF16) for i in range(2)]
                pP = [sbt(p2, f"pP{i}", [128, SB_N * 512], BF16) for i in range(2)]
                mscr = sbt(p2, "mscr", [128, LKB])
                nmv = sbt(p2, "nmv", [128, 1])
                nmb = sbt(p2, "nmb", [128, 1], BF16)
                mrow = sbt(p2, "mrow", [1, 128], BF16)
                rl65 = sbt(p2, "rl65", [65, 512])
                rl0 = sbt(p2, "rl0", [1, 512])
                repl = sbt(p2, "repl", [64, 512])
                buf_free = [0, 0]
                buf_ready = [(0, 0), (0, 0)]
                nmb_free = [0]
                mrow_free = [0]
                mrow_done = [0, 0]

                def load_head(h, bi):
                    w0 = (("pe", buf_free[bi]), ("qmrow", b.cnt["qmrow"]))
                    qh = f"qh{bi}"
                    e_dma(b, qh, kbuf[bi][0:HD, :], k_scr[h], waits=w0)
                    for ci in range(LC):
                        e_dma(b, qh, vbuf[bi][:, ci * 65:ci * 65 + HD], v_scr[h, ci])
                    e_dma(b, qh, qbuf[bi][0:HD, :], q_scr[h])
                    c_q = b.cnt[qh]
                    b.op("dve", lambda e, bi=bi: e.memset(kbuf[bi][64:65, :], 1.0),
                         waits=((qh, c_q),), inc="dve")
                    b.op("dve", lambda e, bi=bi: e.memset(
                        vbuf[bi][:, :].rearrange("p (c w) -> p c w", w=65)[:, :, 64:65],
                        1.0), inc="dve")
                    return (c_q, b.cnt["dve"])

                buf_ready[0] = load_head(0, 0)
                for h in range(NH):
                    bi = h % 2
                    if h + 1 < NH:
                        buf_ready[(h + 1) % 2] = load_head(h + 1, (h + 1) % 2)
                    rq, rd = buf_ready[bi]
                    Kb, Vb, Qb = kbuf[bi], vbuf[bi], qbuf[bi]
                    pair, rsel = h // 2, h % 2
                    for (qs_, qw) in ext_blocks:
                        nqt = qw // 128
                        for t in range(nqt):
                            q0 = qs_ + t * 128
                            for kb in range(LKB):
                                bank = kb % 2
                                pa = pf(bank)
                                mm = e_mm(b, pa, Qb[0:HD, q0:q0 + 128],
                                          Kb[0:HD, kb * 512:(kb + 1) * 512],
                                          start=True, stop=True,
                                          waits=((f"qh{bi}", rq), ("dve", rd),
                                                 ("qmrow", mrow_done[bi]))
                                          + b.bank_wait(bank), inc="pe")
                                b.op("dve", lambda e, pa=pa, kb=kb: e.reduce_max(
                                    mscr[:, kb:kb + 1], pa, axis=AX.X),
                                    waits=(("pe", mm),), inc="dve")
                                b.bank_done(bank, "dve")
                            b.op("dve", lambda e: e.reduce_max(
                                nmv[:, :], mscr[:, 0:LKB], axis=AX.X, negate=True))
                            b.op("dve", lambda e: e.tensor_copy(nmb[:, :], nmv[:, :]),
                                 inc="dve", waits=(("pe", nmb_free[0]),))
                            c_nm = b.cnt["dve"]
                            ptm = pbf(0)[0:1, 0:128]
                            tr = e_tr(b, ptm, nmb[:, :], IB,
                                      waits=(("dve", c_nm),) + b.bank_wait(("bf", 0)),
                                      inc="pe")
                            nmb_free[0] = tr
                            cpm = b.op("dve", lambda e: e.tensor_copy(mrow[:, :], ptm),
                                       waits=(("pe", tr), ("qmrow", mrow_free[0])),
                                       inc="dve")
                            b.bankg[("bf", 0)] = (("dve", b.cnt["dve"]),)
                            e_dma(b, "qmrow", Qb[64:65, q0:q0 + 128], mrow[:, :],
                                  waits=(("dve", cpm),))
                            mrow_free[0] = b.cnt["qmrow"]
                        c_qrow = b.cnt["qmrow"]
                        av_ps = psum_f[0:65, 5 * 512:5 * 512 + qw]
                        n_sc = (LC + SB_N - 1) // SB_N
                        first = True
                        for sc in range(n_sc):
                            nch = min(SB_N, LC - sc * SB_N)
                            pb_t = pP[sc % 2]
                            mm_last = None
                            for i in range(nch):
                                cc = sc * SB_N + i
                                ww = ()
                                if i == 0:
                                    ww = (("qmrow", c_qrow),) + b.bank_wait(0, 1, 2, 3, 4)
                                mm_last = e_mm(b, pf(i, qw),
                                               Kb[0:65, cc * 128:(cc + 1) * 128],
                                               Qb[0:65, qs_:qs_ + qw],
                                               start=True, stop=True, waits=ww,
                                               inc="pe" if i == nch - 1 else None)
                            if qw == 512:
                                b.op("act", lambda e, n=nch, o=pP[sc % 2]: e.activation(
                                    o[:, 0:n * 512], psum_f[:, 0:n * 512], AF.Exp),
                                    waits=(("pe", mm_last),), inc="act")
                            else:
                                for i in range(nch):
                                    b.op("act", lambda e, i=i, qw=qw, o=pP[sc % 2]:
                                         e.activation(o[:, i * qw:(i + 1) * qw],
                                                      pf(i, qw), AF.Exp),
                                         waits=(("pe", mm_last),) if i == 0 else (),
                                         inc="act")
                            c_ex = b.cnt["act"]
                            for bk in range(nch):
                                b.bank_done(bk, "act")
                            for i in range(nch):
                                cc = sc * SB_N + i
                                e_mm(b, av_ps, Vb[:, cc * 65:cc * 65 + 65],
                                     pb_t[:, i * qw:(i + 1) * qw],
                                     start=first, stop=(cc == LC - 1),
                                     waits=((("act", c_ex),)
                                            + (b.bank_wait(5) if first else ()))
                                     if i == 0 else (),
                                     inc="pe" if i == nch - 1 else None)
                                first = False
                        c_av = b.cnt["pe"]
                        if "avps" in TAPS and h == 0 and qs_ == HALO:
                            o_avps = nc.dram_tensor("tap_avps", [65, 512], F32,
                                                    kind="ExternalOutput").ap()
                            cps = b.op("dve", lambda e, av_ps=av_ps: e.tensor_copy(
                                stg[0:65, 0:512], av_ps), waits=(("pe", c_av),))
                            e_dma(b, "qtap", o_avps, stg[0:65, 0:512],
                                  waits=(("dve", b.cnt["dve"]),))
                        cr = b.op("dve", lambda e, av_ps=av_ps, qw=qw: e.reciprocal(
                            rl65[64:65, 0:qw], av_ps[64:65, 0:qw]),
                            waits=(("pe", c_av),), inc="dve")
                        dmr = e_dma(b, "qrl", rl0[0:1, 0:qw], rl65[64:65, 0:qw],
                                    waits=(("dve", cr),))
                        rp = pf(0, qw)[0:64, 0:qw]
                        mmr = e_mm(b, rp, ones_row_f[0:1, 0:64], rl0[0:1, 0:qw],
                                   start=True, stop=True,
                                   waits=(("qrl", dmr),) + b.bank_wait(0), inc="pe")
                        ce = b.op("dve", lambda e, rp=rp, qw=qw: e.tensor_copy(
                                  repl[:, 0:qw], rp),
                                  waits=(("pe", mmr),), inc="dve")
                        b.bank_done(0, "dve")
                        b.op("dve", lambda e, av_ps=av_ps, pair=pair, rsel=rsel,
                             qs_=qs_, qw=qw: e.tensor_tensor(
                                 av_fm[pair][rsel * 64:(rsel + 1) * 64, qs_:qs_ + qw],
                                 av_ps[0:64, 0:qw], repl[0:64, 0:qw], AluOpType.mult),
                             waits=(("dve", ce),), inc="dve")
                        b.bank_done(5, "dve")
                    buf_free[bi] = b.cnt["pe"]
                b.barrier()
            tap("av0", av_fm)

            if "skip_p3" in TAPS:
                pass
            elif True:
              with contextlib.ExitStack() as p3:
                n_grp = 2 if n_ext_tiles > 8 else 1
                if n_grp == 2:
                    g_split = (len(ext_blocks) + 1) // 2
                    blk_groups = [ext_blocks[:g_split], ext_blocks[g_split:]]
                else:
                    blk_groups = [ext_blocks]
                max_gt = max(sum(bw for _, bw in g) for g in blk_groups) // 128
                y_acc = [sbt(p3, f"yacc{t}", [128, D]) for t in range(max_gt)]
                combT = sbt(p3, "combT", [E, EXT], BF16)
                ctf = sbt(p3, "ctf", [128, E], BF16)
                be2s = sbt(p3, "be2s", [E, D], BF16)
                ldb = e_dma(b, "qc", be2s[:, :], be2_b,
                            waits=(("dve", b.cnt["dve"]),))
                for t in range(n_ext_tiles):
                    b.op("dve", lambda e, t=t: e.tensor_copy(
                        ctf[:, :], comb[:, t * E:(t + 1) * E]), inc="dve")
                    c1 = b.cnt["dve"]
                    half = t % 2
                    pslot = pbf(half)[0:E, 0:128]
                    tr = e_tr(b, pslot, ctf[:, :], IB,
                              waits=(("dve", c1),) + b.bank_wait(("bf", half)),
                              inc="pe")
                    b.op("dve", lambda e, t=t, p=pslot: e.tensor_copy(
                        combT[:, t * 128:(t + 1) * 128], p), waits=(("pe", tr),),
                        inc="dve")
                    b.bankg[("bf", half)] = (("dve", b.cnt["dve"]),)

                be1s = sbt(p3, "be1s", [128, E * ICH])
                be3s = sbt(p3, "be3s", [128, E * ICH])
                bs1s = sbt(p3, "bs1s", [128, SHCH])
                bs3s = sbt(p3, "bs3s", [128, SHCH])
                for e_i in range(E):
                    for ic in range(ICH):
                        e_dma(b, "qc", be1s[:, e_i * ICH + ic:e_i * ICH + ic + 1],
                              be1_col[e_i, ic])
                        e_dma(b, "qc", be3s[:, e_i * ICH + ic:e_i * ICH + ic + 1],
                              be3_col[e_i, ic])
                for ic in range(SHCH):
                    e_dma(b, "qc", bs1s[:, ic:ic + 1], bs1_col[ic])
                    e_dma(b, "qc", bs3s[:, ic:ic + 1], bs3_col[ic])
                c_bias_ld = b.cnt["qc"]
                w1t = [sbt(p3, f"ew1_{c}", [128, max(INTER, 128)], BF16)
                       for c in range(DCH)]
                w3t = [sbt(p3, f"ew3_{c}", [128, max(INTER, 128)], BF16)
                       for c in range(DCH)]
                w2t = [sbt(p3, f"ew2_{c}", [128, D], BF16)
                       for c in range(max(ICH, 1))]
                hsil = sbt(p3, "hsil", [128, 512])
                het = [sbt(p3, f"het{c}", [128, 512], BF16) for c in range(max(ICH, 1))]
                xres = sbt(p3, "xres", [128, D])
                w_free = [0]
                het_free = [0]
                xres_free = [0]
                for grp in blk_groups:
                  if not grp:
                    continue
                  g0 = grp[0][0] // 128
                  g_tiles = sum(bw for _, bw in grp) // 128
                  for gt in range(g_tiles):
                    t = g0 + gt
                    for half in range(NB):
                        bank = (gt * NB + half) % 4
                        po = pf(bank, OW)
                        for p in range(NPAIR):
                            e_mm(b, po, av_fm[p][:, t * 128:(t + 1) * 128],
                                 wpT[p][:, half * OW:(half + 1) * OW],
                                 start=(p == 0), stop=False,
                                 waits=b.bank_wait(bank) if p == 0 else ())
                        e_mm(b, po, combT[:, t * 128:(t + 1) * 128],
                             be2s[:, half * OW:(half + 1) * OW], start=False,
                             stop=False,
                             waits=(("qc", c_bias_ld), ("dve", b.cnt["dve"]))
                             if (t == g0 and half == 0) else ())
                        e_mm(b, po, ones_row_bf[:, :], bpsr[:, half * OW:(half + 1) * OW],
                             start=False, stop=True, inc="pe")
                        c_pe = b.cnt["pe"]
                        b.op("dve", lambda e, po=po, gt=gt, half=half: e.tensor_copy(
                            y_acc[gt][:, half * OW:(half + 1) * OW], po),
                            waits=(("pe", c_pe), ("qst", b.cnt["qst"])), inc="dve")
                        b.bank_done(bank, "dve")

                  def ffn_unit(n_ich, w1src, w3src, w2src, b1c, b3c, c_col_fn,
                               grp=grp, g0=g0):
                    for c in range(DCH):
                        e_dma(b, "qwu", w1t[c][:, 0:n_ich * 128], w1src(c),
                              waits=(("pe", w_free[0]),) if c == 0 else ())
                        e_dma(b, "qwu", w3t[c][:, 0:n_ich * 128], w3src(c))
                    for ic in range(n_ich):
                        e_dma(b, "qwu", w2t[ic][:, :], w2src(ic))
                    c_w = b.cnt["qwu"]
                    for (bs, bw) in grp:
                        for ic in range(n_ich):
                            b1 = pf(4, bw)
                            b3 = pf(5, bw)
                            for c in range(DCH):
                                e_mm(b, b1, w1t[c][:, ic * 128:(ic + 1) * 128],
                                     xn_ext[c][:, bs:bs + bw], start=(c == 0),
                                     stop=(c == DCH - 1),
                                     waits=((("qwu", c_w),) + b.bank_wait(4))
                                     if c == 0 else ())
                            for c in range(DCH):
                                e_mm(b, b3, w3t[c][:, ic * 128:(ic + 1) * 128],
                                     xn_ext[c][:, bs:bs + bw], start=(c == 0),
                                     stop=(c == DCH - 1),
                                     waits=b.bank_wait(5) if c == 0 else (),
                                     inc="pe" if c == DCH - 1 else None)
                            c_pe = b.cnt["pe"]
                            b.op("act", lambda e, b1=b1, ic=ic, bw=bw: e.activation(
                                hsil[:, 0:bw], b1, AF.Silu, bias=b1c(ic)),
                                waits=(("pe", c_pe), ("dve", b.cnt["dve"]),
                                       ("qc", c_bias_ld)), inc="act")
                            c_si = b.cnt["act"]
                            b.bank_done(4, "act")
                            b.op("dve", lambda e, b3=b3, ic=ic, bw=bw:
                                 e.scalar_tensor_tensor(
                                     het[ic][:, 0:bw], b3, b3c(ic), hsil[:, 0:bw],
                                     AluOpType.add, AluOpType.mult),
                                 waits=(("act", c_si), ("pe", het_free[0])),
                                 inc="dve")
                            b.bank_done(5, "dve")
                        c_h = b.cnt["dve"]
                        nt = bw // 128
                        for tt in range(nt):
                            ti = (bs + tt * 128) // 128
                            gt = ti - g0
                            for half in range(NB):
                                bank = (gt * NB + half) % 4
                                po = pf(bank, OW)
                                for ic in range(n_ich):
                                    e_mm(b, po, het[ic][:, tt * 128:(tt + 1) * 128],
                                         w2t[ic][:, half * OW:(half + 1) * OW],
                                         start=(ic == 0), stop=(ic == n_ich - 1),
                                         waits=((("dve", c_h),) + b.bank_wait(bank))
                                         if ic == 0 else (),
                                         inc="pe" if ic == n_ich - 1 else None)
                                c_pe = b.cnt["pe"]
                                ccol = c_col_fn(ti)
                                if ccol is None:
                                    b.op("dve", lambda e, po=po, gt=gt, half=half:
                                         e.tensor_tensor(
                                             y_acc[gt][:, half * OW:(half + 1) * OW],
                                             y_acc[gt][:, half * OW:(half + 1) * OW],
                                             po, AluOpType.add),
                                         waits=(("pe", c_pe),), inc="dve")
                                else:
                                    b.op("dve", lambda e, po=po, gt=gt, half=half,
                                         ccol=ccol: e.scalar_tensor_tensor(
                                             y_acc[gt][:, half * OW:(half + 1) * OW],
                                             po, ccol,
                                             y_acc[gt][:, half * OW:(half + 1) * OW],
                                             AluOpType.mult, AluOpType.add),
                                         waits=(("pe", c_pe),), inc="dve")
                                b.bank_done(bank, "dve")
                        het_free[0] = b.cnt["pe"]
                    w_free[0] = b.cnt["pe"]

                  for e_i in range(E):
                    ffn_unit(
                        ICH,
                        lambda c, e_i=e_i: we1_b[e_i, c * 128:(c + 1) * 128, :],
                        lambda c, e_i=e_i: we3_b[e_i, c * 128:(c + 1) * 128, :],
                        lambda ic, e_i=e_i: we2_b[e_i, ic * 128:(ic + 1) * 128, :],
                        lambda ic, e_i=e_i: be1s[:, e_i * ICH + ic:e_i * ICH + ic + 1],
                        lambda ic, e_i=e_i: be3s[:, e_i * ICH + ic:e_i * ICH + ic + 1],
                        lambda ti, e_i=e_i: comb[:, ti * E + e_i: ti * E + e_i + 1],
                    )
                  n_sh_units = max(1, SHCH // max(ICH, 1))
                  shw = SHCH // n_sh_units
                  for u in range(n_sh_units):
                    ffn_unit(
                        shw,
                        lambda c, u=u: ws1_b[c * 128:(c + 1) * 128,
                                             u * shw * 128:(u + 1) * shw * 128],
                        lambda c, u=u: ws3_b[c * 128:(c + 1) * 128,
                                             u * shw * 128:(u + 1) * shw * 128],
                        lambda ic, u=u: ws2_b[(u * shw + ic) * 128:
                                              (u * shw + ic + 1) * 128, :],
                        lambda ic, u=u: bs1s[:, u * shw + ic:u * shw + ic + 1],
                        lambda ic, u=u: bs3s[:, u * shw + ic:u * shw + ic + 1],
                        lambda ti: None,
                    )

                  for gt in range(g_tiles):
                    t = g0 + gt
                    ld = e_dma(b, "qxr", xres[:, :], x_ext[t * 128:(t + 1) * 128, :],
                               waits=(("dve", xres_free[0]),))
                    b.op("dve", lambda e, gt=gt: e.tensor_tensor(
                        y_acc[gt][:, :], y_acc[gt][:, :], xres[:, :], AluOpType.add),
                        waits=(("qxr", ld),), inc="dve")
                    xres_free[0] = b.cnt["dve"]
                    c_d = b.cnt["dve"]
                    e_dma(b, "qst", x1_scr[t * 128:(t + 1) * 128, :], y_acc[gt][:, :],
                          waits=(("dve", c_d),))
                    if mode == "l0" and t * 128 >= HALO:
                        r0 = t * 128 - HALO
                        e_dma(b, "qst", out_x1[r0:r0 + 128, :], y_acc[gt][:, :])
                b.barrier()
            l0s.close()

        if do_l1:
            with contextlib.ExitStack() as p4:
                ln = make_ln_sb(p4, IB, pre="p4")
                for (bs, bw) in ext_blocks:
                    emit_ln(ln, lambda t, bs=bs: x1_src[bs + t * 128:bs + (t + 1) * 128, :],
                            bw // 128, xn_ext, bs)
                b.barrier()

            av1 = [sbt(b.es, f"av1_{p}", [128, OWN], BF16) for p in range(NPAIR)]
            p56 = contextlib.ExitStack()
            q1 = [sbt(p56, f"q1_{p}", [128, OWN], BF16) for p in range(NPAIR)]
            k1 = [sbt(p56, f"k1_{p}", [128, EXT], BF16) for p in range(NPAIR)]
            v1 = [sbt(p56, f"v1_{t}", [128, D], BF16) for t in range(n_ext_tiles)]

            with contextlib.ExitStack() as p5:
                ipT = [sbt(p5, f"ipT{c}", [128, D], BF16) for c in range(DCH)]
                qk_tm = sbt(p5, "qk_tm", [128, D], BF16)
                ro1 = sbt(p5, "ro1", [128, D], BF16)
                rc = sbt(p5, "rc", [128, D], BF16)
                rs = sbt(p5, "rs", [128, D], BF16)
                rope_free = [0]

                def qkv_pass(which):
                    b.barrier()
                    prep_T(lambda oc, which=which:
                           w_in[which * D + oc * 128:which * D + (oc + 1) * 128, :],
                           DCH, ipT, stg, stg_bf, IB, D)
                    b.barrier()
                    tiles = range(n_own_tiles) if which == 0 else range(n_ext_tiles)
                    for t in tiles:
                        e0 = t * 128 + (HALO if which == 0 else 0)
                        for half in range(NB):
                            bank = half % 2
                            po = pf(bank, OW)
                            for c in range(DCH):
                                e_mm(b, po, xn_ext[c][:, e0:e0 + 128],
                                     ipT[c][:, half * OW:(half + 1) * OW],
                                     start=(c == 0), stop=(c == DCH - 1),
                                     waits=b.bank_wait(bank) if c == 0 else (),
                                     inc="pe" if c == DCH - 1 else None)
                            c_pe = b.cnt["pe"]
                            if which == 2:
                                b.op("dve", lambda e, po=po, t=t, half=half:
                                     e.tensor_copy(
                                         v1[t][:, half * OW:(half + 1) * OW], po),
                                     waits=(("pe", c_pe),), inc="dve")
                            else:
                                b.op("dve", lambda e, po=po, half=half: e.tensor_copy(
                                    qk_tm[:, half * OW:(half + 1) * OW], po),
                                    waits=(("pe", c_pe), ("pe", rope_free[0])),
                                    inc="dve")
                            b.bank_done(bank, "dve")
                        if which == 2:
                            continue
                        cos_t = ropeq_cos if which == 0 else ropek_cos
                        sin_t = ropeq_sin if which == 0 else ropek_sin
                        l1_ = e_dma(b, "qrope", rc[:, :], cos_t[e0:e0 + 128, :],
                                    waits=(("dve", b.cnt["dve"]),))
                        l2_ = e_dma(b, "qrope", rs[:, :], sin_t[e0:e0 + 128, :])
                        b.op("dve", lambda e: e.tensor_tensor(
                            ro1[:, :], qk_tm[:, :], rc[:, :], AluOpType.mult),
                            waits=(("qrope", l2_),))
                        swp = qk_tm[:, :].rearrange("p (a two) -> p a two", two=2)[:, :, ::-1]
                        b.op("dve", lambda e, swp=swp: e.tensor_tensor(
                            rc[:, :].rearrange("p (a two) -> p a two", two=2), swp,
                            rs[:, :].rearrange("p (a two) -> p a two", two=2),
                            AluOpType.mult))
                        b.op("dve", lambda e: e.tensor_tensor(
                            ro1[:, :], ro1[:, :], rc[:, :], AluOpType.add), inc="dve")
                        c_ro = b.cnt["dve"]
                        dst = q1 if which == 0 else k1
                        for c in range(DCH):
                            halfb = c % 2
                            pslot = pbf(halfb)[:, 0:128]
                            tr = e_tr(b, pslot, ro1[:, c * 128:(c + 1) * 128], IB,
                                      waits=(("dve", c_ro),) + b.bank_wait(("bf", halfb)),
                                      inc="pe")
                            b.op("dve", lambda e, p=pslot, c=c, t=t, dst=dst:
                                 e.tensor_copy(dst[c][:, t * 128:(t + 1) * 128], p),
                                 waits=(("pe", tr),), inc="dve")
                            b.bankg[("bf", halfb)] = (("dve", b.cnt["dve"]),)
                        rope_free[0] = b.cnt["pe"]

                qkv_pass(1)
                qkv_pass(0)
                qkv_pass(2)
                b.barrier()
            tap("xn1", xn_ext)
            tap("q1", q1)
            tap("k1", k1)
            tap("v1", v1)

            with contextlib.ExitStack() as p6:
                msk = sbt(p6, "msk", [128, NKC * 128])
                Ps = sbt(p6, "Ps", [128, NKC * 128])
                Pn = sbt(p6, "Pn", [128, NKC * 128], BF16)
                PtT = sbt(p6, "PtT", [128, NKC * 128], BF16)
                nm1 = sbt(p6, "nm1", [128, 1])
                l1c = sbt(p6, "l1c", [128, 1])
                rl1 = sbt(p6, "rl1", [128, 1])
                for qt in range(n_own_tiles):
                    ldm = e_dma(b, "qmsk", msk[:, :], masks[qt],
                                waits=(("dve", b.cnt["dve"]),))
                    for h in range(NH):
                        pair, rsel = h // 2, h % 2
                        bank = h % 2
                        ps_ = pf(bank, NKC * 128)
                        e_mm(b, ps_, q1[pair][rsel * 64:(rsel + 1) * 64,
                                              qt * 128:(qt + 1) * 128],
                             k1[pair][rsel * 64:(rsel + 1) * 64,
                                      qt * 128:qt * 128 + NKC * 128],
                             start=True, stop=True,
                             waits=(("qmsk", ldm),) + b.bank_wait(bank), inc="pe")
                        c_pe = b.cnt["pe"]
                        b.op("dve", lambda e, ps_=ps_: e.tensor_tensor(
                            ps_, ps_, msk[:, :], AluOpType.add),
                            waits=(("pe", c_pe),), inc="dve")
                        b.op("dve", lambda e, ps_=ps_: e.reduce_max(
                            nm1[:, :], ps_, axis=AX.X, negate=True), inc="dve")
                        c_nm = b.cnt["dve"]
                        b.op("act", lambda e, ps_=ps_: e.activation(
                            Ps[:, :], ps_, AF.Exp, bias=nm1[:, :],
                            accum_out=l1c[:, :]),
                            waits=(("dve", c_nm),), inc="act")
                        c_ex = b.cnt["act"]
                        b.bank_done(bank, "act")
                        b.op("dve", lambda e: e.reciprocal(rl1[:, :], l1c[:, :]),
                             waits=(("act", c_ex),))
                        b.op("dve", lambda e: e.tensor_scalar(
                            Pn[:, :], Ps[:, :], rl1[:, :], None, AluOpType.mult),
                            inc="dve")
                        c_pn = b.cnt["dve"]
                        for i in range(NKC):
                            halfb = i % 2
                            pslot = pbf(halfb)[:, 0:128]
                            tr = e_tr(b, pslot, Pn[:, i * 128:(i + 1) * 128], IB,
                                      waits=(("dve", c_pn),) + b.bank_wait(("bf", halfb)),
                                      inc="pe")
                            b.op("dve", lambda e, p=pslot, i=i: e.tensor_copy(
                                PtT[:, i * 128:(i + 1) * 128], p),
                                waits=(("pe", tr),), inc="dve")
                            b.bankg[("bf", halfb)] = (("dve", b.cnt["dve"]),)
                        c_pt = b.cnt["dve"]
                        avb = pf(4, 128)[0:64, 0:128]
                        for i in range(NKC):
                            e_mm(b, avb, v1[qt + i][:, h * HD:(h + 1) * HD],
                                 PtT[:, i * 128:(i + 1) * 128],
                                 start=(i == 0), stop=(i == NKC - 1),
                                 waits=((("dve", c_pt),) + b.bank_wait(4))
                                 if i == 0 else (),
                                 inc="pe" if i == NKC - 1 else None)
                        c_avp = b.cnt["pe"]
                        b.op("dve", lambda e, pair=pair, rsel=rsel, qt=qt, avb=avb:
                             e.tensor_copy(av1[pair][rsel * 64:(rsel + 1) * 64,
                                                     qt * 128:(qt + 1) * 128], avb),
                             waits=(("pe", c_avp),), inc="dve")
                        b.bank_done(4, "dve")
                b.barrier()
            p56.close()
            tap("av1", av1)

            with contextlib.ExitStack() as p7:
                woT = [sbt(p7, f"woT{c}", [128, D], BF16) for c in range(DCH)]
                prep_T(lambda oc: w_out[oc * 128:(oc + 1) * 128, :], DCH, woT,
                       stg, stg_bf, IB, D)
                tap("woT", woT)
                alpha = sbt(p7, "alpha", [128, 1])
                alph_p = sbt(p7, "alph_p", [128, 1])
                wstage = sbt(p7, "wstage", [128, 128], BF16)

                for hc in range(H1CH):
                    ld = e_dma(b, "qc", stg[:, 0:D], w1[hc * 128:(hc + 1) * 128, :],
                               waits=(("act", b.cnt["act"]), ("qst", b.cnt["qst"]),
                                      ("dve", b.cnt["dve"])))
                    b.op("dve", lambda e: e.tensor_reduce(
                        alpha[:, :], stg[:, 0:D], axis=AX.X, op=AluOpType.add,
                        apply_absolute_value=True), waits=(("qc", ld),))
                    b.op("dve", lambda e: e.tensor_scalar(
                        alpha[:, :], alpha[:, :], 1.0 / D, None, AluOpType.mult),
                        inc="dve")
                    c_al = b.cnt["dve"]
                    b.op("act", lambda e: e.activation(
                        stg_bf[:, 0:D], stg[:, 0:D], AF.Sign),
                        waits=(("dve", c_al), ("pe", b.cnt["pe"])), inc="act")
                    c_sg = b.cnt["act"]
                    cb = b.op("dve", lambda e: e.tensor_scalar(
                        stg_bf[:, 0:D], stg_bf[:, 0:D], alpha[:, :], None,
                        AluOpType.mult), inc="dve", waits=(("act", c_sg),))
                    for dc in range(DCH):
                        halfb = dc % 2
                        pslot = pbf(halfb)[:, 0:128]
                        tr = e_tr(b, pslot, stg_bf[:, dc * 128:(dc + 1) * 128], IB,
                                  waits=(("dve", cb), ("qst", b.cnt["qst"]))
                                  + b.bank_wait(("bf", halfb)), inc="pe")
                        cw = b.op("dve", lambda e, p=pslot: e.tensor_copy(
                            wstage[:, :], p),
                            waits=(("pe", tr), ("qst", b.cnt["qst"])), inc="dve")
                        b.bankg[("bf", halfb)] = (("dve", b.cnt["dve"]),)
                        e_dma(b, "qst", w1t_scr[dc, :, hc * 128:(hc + 1) * 128],
                              wstage[:, :], waits=(("dve", cw),))

                H1C2 = max(1, H1 // 2048)
                for oc in range(DCH):
                    for j in range(H1C2):
                        cw_ = min(2048, H1 - j * 2048)
                        ld = e_dma(b, "qc", stg[:, 0:cw_],
                                   w2[oc * 128:(oc + 1) * 128, j * 2048:j * 2048 + cw_],
                                   waits=(("act", b.cnt["act"]), ("qst", b.cnt["qst"]),
                                          ("dve", b.cnt["dve"])))
                        b.op("dve", lambda e, cw_=cw_: e.tensor_reduce(
                            alph_p[:, :], stg[:, 0:cw_], axis=AX.X, op=AluOpType.add,
                            apply_absolute_value=True), waits=(("qc", ld),))
                        if j == 0:
                            b.op("dve", lambda e: e.tensor_copy(alpha[:, :], alph_p[:, :]))
                        else:
                            b.op("dve", lambda e: e.tensor_tensor(
                                alpha[:, :], alpha[:, :], alph_p[:, :], AluOpType.add))
                        b.op("act", lambda e, j=j, cw_=cw_: e.activation(
                            stg_bf[:, j * 2048:j * 2048 + cw_], stg[:, 0:cw_], AF.Sign),
                            inc="act",
                            waits=(("qc", ld),) + ((("pe", b.cnt["pe"]),) if j == 0 else ()))
                    b.op("dve", lambda e: e.tensor_scalar(
                        alpha[:, :], alpha[:, :], 1.0 / H1, None, AluOpType.mult),
                        inc="dve")
                    c_al = b.cnt["dve"]
                    cb = b.op("dve", lambda e: e.tensor_scalar(
                        stg_bf[:, 0:H1], stg_bf[:, 0:H1], alpha[:, :], None,
                        AluOpType.mult), inc="dve",
                        waits=(("act", b.cnt["act"]), ("dve", c_al)))
                    for hc in range(H1CH):
                        halfb = hc % 2
                        pslot = pbf(halfb)[:, 0:128]
                        tr = e_tr(b, pslot, stg_bf[:, hc * 128:(hc + 1) * 128], IB,
                                  waits=(("dve", cb), ("qst", b.cnt["qst"]))
                                  + b.bank_wait(("bf", halfb)), inc="pe")
                        cw = b.op("dve", lambda e, p=pslot: e.tensor_copy(
                            wstage[:, :], p),
                            waits=(("pe", tr), ("qst", b.cnt["qst"])), inc="dve")
                        b.bankg[("bf", halfb)] = (("dve", b.cnt["dve"]),)
                        e_dma(b, "qst", w2t_scr[hc, :, oc * 128:(oc + 1) * 128],
                              wstage[:, :], waits=(("dve", cw),))
                b.barrier()

                b1s = sbt(p7, "b1s", [128, H1CH])
                for hc in range(H1CH):
                    e_dma(b, "qc", b1s[:, hc:hc + 1], b1_col[hc],
                          waits=(("dve", b.cnt["dve"]),) if hc == 0 else ())
                b2r = sbt(p7, "b2r", [1, D], BF16)
                ldb2 = e_dma(b, "qc", stg[0:1, 0:D], b2_row,
                             waits=(("dve", b.cnt["dve"]), ("act", b.cnt["act"])))
                b.op("dve", lambda e: e.tensor_copy(b2r[:, :], stg[0:1, 0:D]),
                     waits=(("qc", ldb2),), inc="dve")
                h1a = [sbt(p7, f"h1a{i}", [128, 512], BF16) for i in range(H1CH)]
                w1tl = [[sbt(p7, f"w1tl{i}_{c}", [128, 128], BF16) for c in range(DCH)]
                        for i in range(2)]
                w2tl = [sbt(p7, f"w2tl{i}", [128, OW], BF16) for i in range(2)]
                xr1 = sbt(p7, "xr1", [128, D])
                ot = sbt(p7, "ot", [128, D])
                h1a_free = [0]
                w1tl_free = [0, 0]
                w2tl_free = [0, 0]
                xr1_free = [0]
                ot_free = [0]
                c_b1 = b.cnt["qc"]

                for (bs, bw) in own_blocks:
                    nt = bw // 128
                    for hc in range(H1CH):
                        wi = hc % 2
                        qwname = f"qwa{wi}"
                        for dc in range(DCH):
                            e_dma(b, qwname, w1tl[wi][dc][:, :],
                                  w1t_scr[dc, :, hc * 128:(hc + 1) * 128],
                                  waits=(("pe", w1tl_free[wi]),) if dc == 0 else ())
                        c_w = b.cnt[qwname]
                        bank = 4 + (hc % 2)
                        pb = pf(bank, bw)
                        for dc in range(DCH):
                            e_mm(b, pb, w1tl[wi][dc][:, :],
                                 xn_ext[dc][:, HALO + bs:HALO + bs + bw],
                                 start=(dc == 0), stop=(dc == DCH - 1),
                                 waits=(((f"qwa{wi}", c_w),) + b.bank_wait(bank))
                                 if dc == 0 else (),
                                 inc="pe" if dc == DCH - 1 else None)
                        w1tl_free[wi] = b.cnt["pe"]
                        c_pe = b.cnt["pe"]
                        b.op("act", lambda e, pb=pb, hc=hc, bw=bw: e.activation(
                            h1a[hc][:, 0:bw], pb, AF.Gelu, bias=b1s[:, hc:hc + 1]),
                            waits=(("pe", c_pe), ("qc", c_b1), ("pe", h1a_free[0])),
                            inc="act")
                        b.bank_done(bank, "act")
                    c_h1 = b.cnt["act"]
                    for half in range(NB):
                        for t in range(nt):
                            bank = t % 4
                            po = pf(bank, OW)
                            for p in range(NPAIR):
                                e_mm(b, po, av1[p][:, bs + t * 128:bs + (t + 1) * 128],
                                     woT[p][:, half * OW:(half + 1) * OW],
                                     start=(p == 0), stop=False,
                                     waits=b.bank_wait(bank) if p == 0 else (),
                                     inc="pe" if p == NPAIR - 1 else None)
                        for hc in range(0 if "skip_ffn1" in TAPS else H1CH):
                            wi = hc % 2
                            ldw = e_dma(b, f"qwb{wi}", w2tl[wi][:, :],
                                        w2t_scr[hc, :, half * OW:(half + 1) * OW],
                                        waits=(("pe", w2tl_free[wi]),))
                            for t in range(nt):
                                bank = t % 4
                                po = pf(bank, OW)
                                e_mm(b, po, h1a[hc][:, t * 128:(t + 1) * 128],
                                     w2tl[wi][:, :], start=False, stop=False,
                                     waits=((f"qwb{wi}", ldw), ("act", c_h1))
                                     if t == 0 else (),
                                     inc="pe" if t == nt - 1 else None)
                            w2tl_free[wi] = b.cnt["pe"]
                        for t in range(nt):
                            bank = t % 4
                            po = pf(bank, OW)
                            e_mm(b, po, ones_row_bf[:, :],
                                 b2r[:, half * OW:(half + 1) * OW],
                                 start=False, stop=True, inc="pe")
                            c_pe = b.cnt["pe"]
                            ldx = e_dma(b, "qxr", xr1[:, half * OW:(half + 1) * OW],
                                        x1_src[HALO + bs + t * 128:
                                               HALO + bs + (t + 1) * 128,
                                               half * OW:(half + 1) * OW],
                                        waits=(("dve", xr1_free[0]),))
                            b.op("dve", lambda e, po=po, half=half: e.tensor_tensor(
                                ot[:, half * OW:(half + 1) * OW], po,
                                xr1[:, half * OW:(half + 1) * OW], AluOpType.add),
                                waits=(("pe", c_pe), ("qxr", ldx),
                                       ("qout", ot_free[0])), inc="dve")
                            xr1_free[0] = b.cnt["dve"]
                            b.bank_done(bank, "dve")
                            e_dma(b, "qout", out_fin[bs + t * 128:bs + (t + 1) * 128,
                                                     half * OW:(half + 1) * OW],
                                  ot[:, half * OW:(half + 1) * OW],
                                  waits=(("dve", b.cnt["dve"]),))
                            ot_free[0] = b.cnt["qout"]
                    h1a_free[0] = b.cnt["pe"]
        b.barrier()

        with nc.Block() as block:
            b.finalize(block)
    return nc, b



def make_rope_tables(cfg, g0):
    D = cfg["D"]; HD = cfg["HD"]; NH = cfg["NH"]
    EXT = cfg["OWN"] + cfg["HALO"]
    ds = np.arange(HD // 2, dtype=np.float64)
    freqs = np.exp(ds * (-np.log(10000.0) * 2.0 / HD))
    ts = np.arange(g0, g0 + EXT, dtype=np.float64)
    ang = ts[:, None] * freqs[None, :]
    cos = np.cos(ang)
    sin = np.sin(ang)
    cosdup = np.repeat(cos, 2, axis=1)
    sindup = np.empty((EXT, HD))
    sindup[:, 0::2] = -sin
    sindup[:, 1::2] = sin
    cosd = np.tile(cosdup, (1, NH))
    sind = np.tile(sindup, (1, NH))
    return cosd, sind


def make_masks(cfg, g0):
    OWN = cfg["OWN"]; HALO = cfg["HALO"]; CTX = cfg["CTX"]
    NKC = HALO // 128 + 1
    nt = OWN // 128
    out = np.full((nt, 128, NKC * 128), NEG, dtype=np.float32)
    for qt in range(nt):
        gq = g0 + qt * 128 + np.arange(128)
        gk = g0 - HALO + qt * 128 + np.arange(NKC * 128)
        delta = gq[:, None] - gk[None, :]
        ok = (delta >= 0) & (delta < CTX) & (gk[None, :] >= 0)
        out[qt][ok] = 0.0
    return out


def host_inputs_l0(cfg, x_batch, x_ext_np, W):
    D = cfg["D"]; NH = cfg["NH"]; HD = cfg["HD"]
    NPAIR = NH // 2
    ICH = cfg["INTER"] // 128
    SHCH = cfg["SH_I"] // 128
    E = cfg["E"]
    d = dict(
        ident_bf=np.eye(128, dtype=np.float32).astype(BF),
        ones_row_bf=np.ones((1, 128), np.float32).astype(BF),
        ones_row_f=np.ones((1, 128), np.float32),
        x_full=np.ascontiguousarray(x_batch, np.float32),
        x_ext=np.ascontiguousarray(x_ext_np, np.float32),
        wqkv=W["attn_Wqkv"],
        bq8_col=np.ascontiguousarray(
            W["attn_bqkv"][0:NH].reshape(NPAIR, 128, 1), np.float32),
        bk_col=np.ascontiguousarray(
            W["attn_bqkv"][NH:2 * NH].reshape(NPAIR, 128, 1), np.float32),
        bv_row=np.ascontiguousarray(
            W["attn_bqkv"][2 * NH:3 * NH].reshape(1, NH * HD), np.float32),
        wp=W["attn_Wp"],
        bp_row=W["attn_bp"].reshape(1, D),
        gate_w=W["ffn_gate_w"],
        gb_rep=np.ascontiguousarray(
            np.tile(W["ffn_gate_b"].reshape(1, E), (128, 1)), np.float32),
        we1=W["ffn_We1"], we3=W["ffn_We3"], we2=W["ffn_We2"],
        be1_col=np.ascontiguousarray(
            W["ffn_be1"].reshape(E, ICH, 128, 1), np.float32),
        be3_col=np.ascontiguousarray(
            W["ffn_be3"].reshape(E, ICH, 128, 1), np.float32),
        be2=W["ffn_be2"],
        ws1=W["ffn_ws1"], ws3=W["ffn_ws3"], ws2=W["ffn_ws2"],
        bs1_col=np.ascontiguousarray(W["ffn_bs1"].reshape(SHCH, 128, 1), np.float32),
        bs3_col=np.ascontiguousarray(W["ffn_bs3"].reshape(SHCH, 128, 1), np.float32),
        bs2_row=W["ffn_bs2"].reshape(1, D),
    )
    return {k: np.ascontiguousarray(v) for k, v in d.items()}


def host_inputs_l1(cfg, W, g0_own):
    D = cfg["D"]
    H1CH = cfg["H1"] // 128
    qc, qs = make_rope_tables(cfg, g0_own - cfg["HALO"])
    d = dict(
        ident_bf=np.eye(128, dtype=np.float32).astype(BF),
        ones_row_bf=np.ones((1, 128), np.float32).astype(BF),
        ones_row_f=np.ones((1, 128), np.float32),
        in_proj_w=W["attn_in_proj_w"],
        out_proj_w=W["attn_out_proj_w"],
        ropeq_cos=np_bf(qc * (cfg["HD"] ** -0.5)),
        ropeq_sin=np_bf(qs * (cfg["HD"] ** -0.5)),
        ropek_cos=np_bf(qc),
        ropek_sin=np_bf(qs),
        masks=make_masks(cfg, g0_own),
        w1=W["ffn_w1"],
        b1_col=np.ascontiguousarray(W["ffn_b1"].reshape(H1CH, 128, 1), np.float32),
        w2=W["ffn_w2"],
        b2_row=W["ffn_b2"].reshape(1, D),
    )
    return {k: np.ascontiguousarray(v) for k, v in d.items()}


def kernel(**inputs):
    cfg = real_cfg()
    D = cfg["D"]; L = cfg["L"]; OWN = cfg["OWN"]; HALO = cfg["HALO"]
    B = 2
    n_slice = 4
    W0 = {k[3:]: np.asarray(v, np.float32) for k, v in inputs.items()
          if k.startswith("L0_")}
    W1 = {k[3:]: np.asarray(v, np.float32) for k, v in inputs.items()
          if k.startswith("L1_")}
    x = np.asarray(inputs["L0_attn_x"], np.float32)

    def ext_slice(arr_seq, s):
        s0 = s * OWN
        out = np.zeros((OWN + HALO, D), np.float32)
        lo = max(0, s0 - HALO)
        out[HALO - (s0 - lo):] = arr_seq[lo:s0 + OWN]
        return out

    nc0, _ = build_program(cfg, "l0")
    in_maps = []
    for core in range(8):
        bb, ss = core // n_slice, core % n_slice
        in_maps.append(host_inputs_l0(cfg, x[bb], ext_slice(x[bb], ss), W0))
    res0 = run_bass_kernel_spmd(nc0, in_maps, core_ids=list(range(8)))
    x1 = np.zeros((B, L, D), np.float32)
    for core in range(8):
        bb, ss = core // n_slice, core % n_slice
        x1[bb, ss * OWN:(ss + 1) * OWN] = res0.results[core]["x1_own"]

    nc1, _ = build_program(cfg, "l1")
    in_maps = []
    for core in range(8):
        bb, ss = core // n_slice, core % n_slice
        m = host_inputs_l1(cfg, W1, ss * OWN)
        m["x1_ext"] = ext_slice(x1[bb], ss)
        in_maps.append(m)
    res1 = run_bass_kernel_spmd(nc1, in_maps, core_ids=list(range(8)))
    out = np.zeros((B, L, D), np.float32)
    for core in range(8):
        bb, ss = core // n_slice, core % n_slice
        out[bb, ss * OWN:(ss + 1) * OWN] = res1.results[core]["out_own"]
    return out
